# version 01
# baseline (speedup 1.0000x reference)
import os
import contextlib
import numpy as np
import ml_dtypes
import concourse.bass as bass
import concourse.mybir as mybir
from concourse.bass_utils import run_bass_kernel_spmd

F32 = mybir.dt.float32
BF16 = mybir.dt.bfloat16
AF = mybir.ActivationFunctionType
AX = mybir.AxisListType
ALU = mybir.AluOpType

SEQ = 2048
D = 1024
H = 16
HID = 4096
NSEQ = 2
NG = 4
GT = 4
NT = NG * GT
FC = D // 128
HC = HID // 128
NGG = NSEQ * NG

STAGE = os.environ.get("MK_STAGE", "full")
SERIAL = os.environ.get("MK_SERIAL", "0")


class Sem:
    def __init__(self, handle):
        self.h = handle
        self.n = 0


def build_nc():
    nc = bass.Bass("TRN2", target_bir_lowering=False)

    x_d = nc.dram_tensor("x", [NSEQ * SEQ, D], F32, kind="ExternalInput").ap()
    wqkv_d = nc.dram_tensor("wqkv", [D, 3 * D], F32, kind="ExternalInput").ap()
    w1t_d = nc.dram_tensor("w1t", [D, HID], F32, kind="ExternalInput").ap()
    w2t_d = nc.dram_tensor("w2t", [D, HID], F32, kind="ExternalInput").ap()
    h1t_d = nc.dram_tensor("h1t", [HID, HID], F32, kind="ExternalInput").ap()
    h2t_d = nc.dram_tensor("h2t", [HID, HID], F32, kind="ExternalInput").ap()
    w3t_d = nc.dram_tensor("w3t", [HID, D], F32, kind="ExternalInput").ap()
    s1p_d = nc.dram_tensor("s1p", [128, HC], F32, kind="ExternalInput").ap()
    s2p_d = nc.dram_tensor("s2p", [128, HC], F32, kind="ExternalInput").ap()
    sh1p_d = nc.dram_tensor("sh1p", [128, HC], F32, kind="ExternalInput").ap()
    sh2p_d = nc.dram_tensor("sh2p", [128, HC], F32, kind="ExternalInput").ap()
    s3p_d = nc.dram_tensor("s3p", [128, FC], F32, kind="ExternalInput").ap()
    mask_d = nc.dram_tensor("mask", [128, 128], F32, kind="ExternalInput").ap()
    ident_d = nc.dram_tensor("ident", [128, 128], BF16, kind="ExternalInput").ap()
    out_d = nc.dram_tensor("out", [NSEQ * SEQ, D], F32, kind="ExternalOutput").ap()
    DUMPKV = os.environ.get("MK_DUMP", "0") == "kv"
    if DUMPKV:
        dumpK = nc.dram_tensor("dumpK", [128, FC * SEQ], BF16, kind="ExternalOutput").ap()
        dumpV = nc.dram_tensor("dumpV", [128, NT * 1024], BF16, kind="ExternalOutput").ap()

    c_w1 = nc.dram_tensor("c_w1", [HC, 128, D], BF16, kind="Internal").ap()
    c_w2 = nc.dram_tensor("c_w2", [HC, 128, D], BF16, kind="Internal").ap()
    c_h1 = nc.dram_tensor("c_h1", [HC, 128, HID], BF16, kind="Internal").ap()
    c_h2 = nc.dram_tensor("c_h2", [HC, 128, HID], BF16, kind="Internal").ap()
    c_w3 = nc.dram_tensor("c_w3", [FC, 128, HID], BF16, kind="Internal").ap()

    A = nc.alloc_sbuf_tensor
    mask_s = A("mask_s", [128, 128], F32)[:]
    ident_s = A("ident_s", [128, 128], BF16)[:]
    s1p = A("s1p_s", [128, HC], F32)[:]
    s2p = A("s2p_s", [128, HC], F32)[:]
    sh1p = A("sh1p_s", [128, HC], F32)[:]
    sh2p = A("sh2p_s", [128, HC], F32)[:]
    s3p = A("s3p_s", [128, FC], F32)[:]
    stats = A("stats", [128, 32], F32)[:]
    r3 = A("rstd3", [128, NT], F32)[:]
    b3 = A("bias3", [128, NT], F32)[:]
    ra1 = A("rstd1", [128, NT], F32)[:]
    rb1 = A("bias1", [128, NT], F32)[:]

    K_fm = A("K_fm", [128, FC * SEQ], BF16)[:]
    V_tm = A("V_tm", [128, NT * 1024], BF16)[:]
    lnf = A("lnf", [128, FC * 512], BF16)[:]
    x2w = A("x2w", [128, GT * 1024], BF16)[:]
    lntmp = A("lntmp", [128, 1024], F32)[:]
    xa_t = [A("xa0", [128, 1024], F32)[:], A("xa1", [128, 1024], F32)[:]]
    xr_t = [A("xr0", [128, 1024], F32)[:]]
    tmt = [A("tmt0", [128, 1024], BF16)[:], A("tmt1", [128, 1024], BF16)[:]]
    outt = [A("outt0", [128, 1024], F32)[:]]
    qsl = [A("qsl0", [128, 4096], BF16)[:], A("qsl1", [128, 4096], BF16)[:]]
    wsl = [A("wsl0", [128, 4096], BF16)[:], A("wsl1", [128, 4096], BF16)[:]]
    ktmp = [A("ktmp0", [128, 1024], BF16)[:], A("ktmp1", [128, 1024], BF16)[:]]
    h_in_t = A("h_in", [128, HC * 512], BF16)
    h_out_t = A("h_out", [128, HC * 512], BF16)

    hi = h_in_t[:]
    Qg = hi[:, 0:4096]
    Pw = [hi[:, 4096:4608], hi[:, 4608:5120]]
    PTs = [hi[:, 5120:5632], hi[:, 5632:6144]]
    O_bf = hi[:, 6144:7168]
    ho = h_out_t[:]
    silu_a = [ho[:, 0:1024].bitcast(F32), ho[:, 1024:2048].bitcast(F32)]
    y_fm = ho[:, 2048:6144]
    h_in = hi
    h_out = ho
    junk = lntmp

    assert nc.sbuf_bytes_remaining >= 0, f"SBUF over: {nc.sbuf_bytes_remaining}"

    arena = nc.alloc_psum_tensor("arena", [128, 4096], F32)
    ar = arena[:]
    BANKS = [ar[:, 0:512], ar[:, 512:1024], ar[:, 1024:1536], ar[:, 1536:2048]]
    PTb = [ar[:, 3072:3584].bitcast(BF16), ar[:, 3584:4096].bitcast(BF16)]
    O_ps = ar[:, 512:576]
    SB = 1024

    acc = stats
    col = lambda i: stats[:, i:i + 1]

    ops = {e: [] for e in ("sync", "gpsimd", "tensor", "vector", "scalar")}
    last_wait = {e: {} for e in ops}
    stack = contextlib.ExitStack()
    sm = {}
    for name in ("dx0", "dx1", "dxr", "dout0", "dout1", "dslot0", "dslot1",
                 "dq0", "dq1", "dscr", "dc", "qfree", "pe", "act", "dve"):
        sm[name] = Sem(stack.enter_context(nc.semaphore(name)))
    smdx = [sm["dx0"], sm["dx1"]]
    smdout = [sm["dout0"], sm["dout1"]]
    smdslot = [sm["dslot0"], sm["dslot1"]]
    smdq = [sm["dq0"], sm["dq1"]]
    smdxr, smdscr, smdc = sm["dxr"], sm["dscr"], sm["dc"]
    pe, act, dve = sm["pe"], sm["act"], sm["dve"]
    qfree = sm["qfree"]
    eng_sem = {"tensor": pe, "scalar": act, "vector": dve}
    const_waits = []
    ser_scope = [""]

    def emit(eng, waits, fn, inc=False, dma=None, extra=None):
        lw = last_wait[eng]
        w = []
        allw = list(waits) + const_waits
        ser = (SERIAL == "1"
               or ({"tensor": "pe", "scalar": "act", "vector": "dve"}[eng] in SERIAL
                   if eng in ("tensor", "scalar", "vector") else False)
               or any(t in SERIAL.split(",") for t in ser_scope[0].split()))
        if eng in ("tensor", "scalar", "vector") and ser:
            allw = allw + [(pe, pe.n), (act, act.n), (dve, dve.n)]
        for s, v in allw:
            if v is None or v <= 0:
                continue
            if v > lw.get(id(s), 0):
                w.append((s.h, v))
                lw[id(s)] = v
        tgt = None
        n = None
        if inc:
            s = eng_sem[eng]
            s.n += 1
            n = s.n
            tgt = (s.h, 1)
        elif dma is not None:
            dma.n += 16
            n = dma.n
            tgt = (dma.h, 16)

        xt = []
        if extra:
            for s2 in extra:
                s2.n += 1
                xt.append((s2.h, 1))

        def closure(e, w=w, fn=fn, tgt=tgt, xt=xt):
            for hh, vv in w:
                e.wait_ge(hh, vv)
            inst = fn(e)
            if tgt is not None:
                inst = inst.then_inc(*tgt)
            for hh, vv in xt:
                inst = inst.then_inc(hh, vv)

        ops[eng].append(closure)
        return n

    for dst, src in ((mask_s, mask_d), (ident_s, ident_d), (s1p, s1p_d),
                     (s2p, s2p_d), (sh1p, sh1p_d), (sh2p, sh2p_d), (s3p, s3p_d)):
        emit("sync", [], lambda e, d=dst, s=src: e.dma_start(out=d, in_=s), dma=smdc)
    const_waits.append((smdc, smdc.n))

    scr_total = [0]
    scr_idx = {}

    def scratch_block(cdst, wsrc, m, nk):
        scr_idx[(id(cdst), m)] = len(scr_idx)
        emit("gpsimd", [],
             lambda e, cd=cdst, ws=wsrc, m=m, nk=nk: e.dma_start(
                 out=cd[m].rearrange("p (c j) -> p c j", c=nk),
                 in_=ws[:, m * 128:(m + 1) * 128].rearrange("(c p) j -> p c j", p=128)),
             dma=smdscr)

    gsi_ctr = [0]
    qslot_dma = {}

    def _slot_waits():
        gsi = gsi_ctr[0]
        gsi_ctr[0] += 1
        par = gsi % 2
        waits = [(qfree, gsi - 1)] if gsi >= 2 else []
        return gsi, par, waits

    def emit_kv_slot(key, colbase):
        gsi, par, waits = _slot_waits()
        emit("gpsimd", waits,
             lambda e, par=par, colbase=colbase: e.dma_start(
                 out=qsl[par].rearrange("p (c j) -> p c j", c=FC),
                 in_=wqkv_d[:, colbase:colbase + 512].rearrange(
                     "(c p) j -> p c j", p=128)),
             dma=smdq[par])
        qslot_dma[key] = (par, (smdq[par], smdq[par].n))

    def emit_q_slot(key, m0):
        gsi, par, waits = _slot_waits()
        for b in range(4):
            m = m0 + b
            emit("gpsimd", waits if b == 0 else [],
                 lambda e, par=par, b=b, m=m: e.dma_start(
                     out=qsl[par][:, b * 1024:(b + 1) * 1024].rearrange(
                         "p (c j) -> p c j", c=FC),
                     in_=wqkv_d[:, m * 128:(m + 1) * 128].rearrange(
                         "(c p) j -> p c j", p=128)),
                 dma=smdq[par])
        qslot_dma[key] = (par, (smdq[par], smdq[par].n))

    def emit_seq_kv_slots(s):
        for g in range(NG):
            emit_kv_slot(("K", s, g, 0), D + 0)
            emit_kv_slot(("K", s, g, 1), D + 512)
            emit_kv_slot(("V", s, g, 0), 2 * D + 0)
            emit_kv_slot(("V", s, g, 1), 2 * D + 512)

    def emit_seq_q_slots(s):
        for g in range(NG):
            emit_q_slot(("Q", s, g, 0), 0)
            emit_q_slot(("Q", s, g, 1), 4)

    emit_seq_kv_slots(0)
    for cw, wt, nk in ((c_w1, w1t_d, FC), (c_w2, w2t_d, FC)):
        for m in range(HC):
            scratch_block(cw, wt, m, nk)
    for cw, wt in ((c_h1, h1t_d), (c_h2, h2t_d)):
        for m in range(HC):
            scratch_block(cw, wt, m, HC)
    for m in range(FC):
        scratch_block(c_w3, w3t_d, m, HC)
    scr_total[0] = smdscr.n
    emit_seq_q_slots(0)
    emit_seq_kv_slots(1)
    emit_seq_q_slots(1)

    wslot_i = [0]
    wslot_free = {}

    def load_wslot(blocks):
        par = wslot_i[0] % 2
        waits = []
        if wslot_i[0] >= 2:
            waits.append((pe, wslot_free[wslot_i[0] - 2]))
        waits.append((smdscr, scr_total[0]))
        if len(blocks) == 1:
            cd, m = blocks[0]
            emit("sync", waits,
                 lambda e, par=par, cd=cd, m=m: e.dma_start(out=wsl[par], in_=cd[m]),
                 dma=smdslot[par])
        else:
            cd, m0 = blocks[0]
            nb = len(blocks)
            emit("sync", waits,
                 lambda e, par=par, cd=cd, m0=m0, nb=nb: e.dma_start(
                     out=wsl[par].rearrange("p (b j) -> p b j", b=nb),
                     in_=cd[m0:m0 + nb].rearrange("b p j -> p b j")),
                 dma=smdslot[par])
        i = wslot_i[0]
        wslot_i[0] += 1
        return par, (smdslot[par], smdslot[par].n), i

    ln_par = [0]

    def ln_stats(src_ap, src_ready, rstd_col, bias_col):
        p8 = 8 + (ln_par[0] % 2) * 8
        ln_par[0] += 1
        ss, nsum, ex2, nm, negvar, varv, lnv = (col(p8 + i) for i in range(7))
        a0 = emit("scalar", src_ready,
                  lambda e, s=src_ap, ss=ss: e.activation(junk, s, AF.Square,
                                                          accum_out=ss),
                  inc=True)
        d0 = emit("vector", src_ready,
                  lambda e, s=src_ap, ns=nsum: e.reduce_sum(ns, s, axis=AX.X,
                                                            negate=True),
                  inc=True)
        emit("vector", [(act, a0)],
             lambda e, ss=ss, ex2=ex2: e.tensor_scalar_mul(ex2, ss, 1.0 / 1024.0),
             inc=True)
        d2 = emit("vector", [(dve, d0)],
                  lambda e, ns=nsum, nm=nm: e.tensor_scalar_mul(nm, ns, 1.0 / 1024.0),
                  inc=True)
        d3 = emit("vector", [(dve, d2)],
                  lambda e, nm=nm, ex2=ex2, nv=negvar: e.scalar_tensor_tensor(
                      nv, nm, nm, ex2, ALU.mult, ALU.subtract),
                  inc=True)
        d4 = emit("vector", [(dve, d3)],
                  lambda e, nv=negvar, vv=varv: e.tensor_scalar(
                      vv, nv, -1.0, 1e-5, ALU.mult, ALU.add),
                  inc=True)
        a1 = emit("scalar", [(dve, d4)],
                  lambda e, vv=varv, lnv=lnv: e.activation(lnv, vv, AF.Ln),
                  inc=True)
        a2 = emit("scalar", [(act, a1)],
                  lambda e, lnv=lnv, r=rstd_col: e.activation(r, lnv, AF.Exp,
                                                              scale=-0.5),
                  inc=True)
        d5 = emit("vector", [(act, a2)],
                  lambda e, nm=nm, r=rstd_col, b=bias_col: e.tensor_mul(b, nm, r),
                  inc=True)
        return a2, d5

    tp_par = [0]
    ptb_free = {}
    tmt_free = {}

    def transpose_tile(src_tile, src_ready, dst_fn):
        outs = []
        last_pe = 0
        for b in range(2):
            par = tp_par[0] % 2
            tp_par[0] += 1
            waits = list(src_ready)
            if par in ptb_free:
                waits.append((dve, ptb_free[par]))
            for j in range(4):
                c = b * 4 + j
                last_pe = emit("tensor", waits if j == 0 else [],
                               lambda e, par=par, j=j, c=c, s=src_tile: e.transpose(
                                   PTb[par][:, j * 128:(j + 1) * 128],
                                   s[:, c * 128:(c + 1) * 128], ident_s),
                               inc=(j == 3))
            dst = dst_fn(b)
            d = emit("vector", [(pe, last_pe)],
                     lambda e, par=par, dst=dst: e.tensor_copy(
                         dst,
                         PTb[par][:, 0:512].rearrange("p (c j) -> p c j", c=4)
                         if len(dst.shape) == 3 else PTb[par][:, 0:512]),
                     inc=True)
            ptb_free[par] = d
            outs.append(d)
        return outs, last_pe

    xa_i = [0]
    xa_free = {}
    last_dadd = [0]

    def load_x_tile(s, qt):
        i = xa_i[0]
        xa_i[0] += 1
        par = i % 2
        waits = [] if i < 2 else [(act, xa_free[i - 2])]
        emit("sync", waits,
             lambda e, par=par, s=s, qt=qt: e.dma_start(
                 out=xa_t[par],
                 in_=x_d[s * SEQ + qt * 128:s * SEQ + qt * 128 + 128, :]),
             dma=smdx[par])
        return i, par, (smdx[par], smdx[par].n)

    def lnx_tile_to_lnf(s, qt, tq, with_stats):
        i, par, dmaw = load_x_tile(s, qt)
        t16 = qt
        if with_stats:
            a2, d5 = ln_stats(xa_t[par], [dmaw, (dve, last_dadd[0])],
                              ra1[:, t16:t16 + 1], rb1[:, t16:t16 + 1])
            waits = [(act, a2), (dve, d5)]
        else:
            waits = [dmaw]
        if par in tmt_free:
            waits.append(tmt_free[par])
        aap = emit("scalar", waits,
                   lambda e, par=par, t16=t16: e.activation(
                       tmt[par], xa_t[par], AF.Identity,
                       bias=rb1[:, t16:t16 + 1], scale=ra1[:, t16:t16 + 1]),
                   inc=True)
        xa_free[i] = aap
        lnfv = lnf.rearrange("p (c t) -> p c t", c=FC)
        outs, lpe = transpose_tile(
            tmt[par], [(act, aap)],
            lambda b, tq=tq, lnfv=lnfv: lnfv[:, b * 4:(b + 1) * 4,
                                             tq * 128:(tq + 1) * 128])
        tmt_free[par] = (pe, lpe)
        return outs

    bank_user = {}
    xr_dma = {}
    xr_free = {}
    out_free = {}
    nout = [0]
    exp_prev = [0]
    prev_row_dve = [0]
    prev_norm = [0]
    prev_epi_act = [0]
    win_free = {}
    pts_free = {}
    x2stat_t = {}
    mmpar = [0]
    ktmp_free = {}
    scat_par = [0]

    for s in range(NSEQ):
        ser_scope[0] = "a1 a1ln"
        scat_dve_max = [0]
        scat_act_max = [0]
        v_evac_max = [0]
        for g in range(NG):
            ser_scope[0] = "a1 a1ln"
            lnf_ready = []
            for tq in range(GT):
                lnf_ready += lnx_tile_to_lnf(s, g * GT + tq, tq, with_stats=True)
            lnf_all = [(dve, max(lnf_ready))]

            ser_scope[0] = "a1 a1k"
            for tq in range(GT):
                tt = g * GT + tq
                kpar = tt % 2
                evacs = []
                for ofb in range(2):
                    spar, sdma = qslot_dma[("K", s, g, ofb)]
                    bi = mmpar[0] % 2
                    mmpar[0] += 1
                    waits = [sdma] + lnf_all + bank_user.get(bi, [])
                    if kpar in ktmp_free:
                        waits = waits + [(dve, ktmp_free[kpar][0])]
                    pe_c = None
                    for kc in range(FC):
                        pe_c = emit("tensor", waits if kc == 0 else [],
                                    lambda e, spar=spar, tq=tq, kc=kc, bi=bi: e.matmul(
                                        BANKS[bi],
                                        lnf[:, kc * 512 + tq * 128:
                                            kc * 512 + (tq + 1) * 128],
                                        qsl[spar][:, kc * 512:(kc + 1) * 512],
                                        start=(kc == 0), stop=(kc == FC - 1)),
                                    inc=(kc == FC - 1))
                    aw = [(pe, pe_c)]
                    if kpar in ktmp_free:
                        aw.append((dve, ktmp_free[kpar][0]))
                    a = emit("scalar", aw,
                             lambda e, kpar=kpar, ofb=ofb, bi=bi: e.activation(
                                 ktmp[kpar][:, ofb * 512:(ofb + 1) * 512],
                                 BANKS[bi], AF.Identity),
                             inc=True)
                    if tq == GT - 1:
                        emit("scalar", [(pe, pe_c)], lambda e: e.nop(),
                             extra=[qfree])
                    bank_user[bi] = [(act, a)]
                    evacs.append(a)
                sdve = sact = 0
                ev_w = [(act, max(evacs))]
                for h in range(H):
                    hp = (h % 2) * 64
                    dstv = K_fm[hp:hp + 64,
                                (h // 2) * 2048:(h // 2 + 1) * 2048].rearrange(
                        "p (d r) -> p d r", r=32)
                    for rl in range(2):
                        r = 2 * tt + rl
                        dst = dstv[:, :, r:r + 1]
                        src = ktmp[kpar][rl * 64:rl * 64 + 64,
                                         h * 64:(h + 1) * 64].rearrange(
                            "p (d b) -> p d b", b=1)
                        engn = "vector" if scat_par[0] % 2 == 0 else "scalar"
                        scat_par[0] += 1
                        if engn == "vector":
                            cnt = emit("vector", ev_w,
                                       lambda e, dst=dst, src=src: e.tensor_copy(
                                           dst, src),
                                       inc=True)
                            sdve = cnt
                        else:
                            cnt = emit("scalar", ev_w,
                                       lambda e, dst=dst, src=src: e.copy(dst, src),
                                       inc=True)
                            sact = cnt
                scat_dve_max[0] = max(scat_dve_max[0], sdve)
                scat_act_max[0] = max(scat_act_max[0], sact)
                ktmp_free[kpar] = (sdve, sact)

            ser_scope[0] = "a1 a1v"
            for ofb in range(2):
                spar, sdma = qslot_dma[("V", s, g, ofb)]
                for tq in range(GT):
                    bi = mmpar[0] % 2
                    mmpar[0] += 1
                    waits = [sdma] + lnf_all + bank_user.get(bi, [])
                    pe_c = None
                    for kc in range(FC):
                        pe_c = emit("tensor", waits if kc == 0 else [],
                                    lambda e, spar=spar, tq=tq, kc=kc, bi=bi: e.matmul(
                                        BANKS[bi],
                                        lnf[:, kc * 512 + tq * 128:
                                            kc * 512 + (tq + 1) * 128],
                                        qsl[spar][:, kc * 512:(kc + 1) * 512],
                                        start=(kc == 0), stop=(kc == FC - 1)),
                                    inc=(kc == FC - 1))
                    a = emit("scalar", [(pe, pe_c)],
                             lambda e, g=g, tq=tq, ofb=ofb, bi=bi: e.activation(
                                 V_tm[:, (g * GT + tq) * 1024 + ofb * 512:
                                      (g * GT + tq) * 1024 + (ofb + 1) * 512],
                                 BANKS[bi], AF.Identity),
                             inc=True)
                    if tq == GT - 1:
                        emit("scalar", [(pe, pe_c)], lambda e: e.nop(),
                             extra=[qfree])
                    bank_user[bi] = [(act, a)]
                    v_evac_max[0] = max(v_evac_max[0], a)

        kt_ready = [(dve, scat_dve_max[0]), (act, scat_act_max[0]),
                    (act, v_evac_max[0])]
        if DUMPKV and s == 0:
            emit("sync", kt_ready,
                 lambda e: e.dma_start(out=dumpK, in_=K_fm), dma=sm["dout0"])
            emit("sync", [],
                 lambda e: e.dma_start(out=dumpV, in_=V_tm), dma=sm["dout0"])

        for g in range(NG):
            gg = s * NG + g
            ser_scope[0] = "a2"
            lnf_ready = []
            for tq in range(GT):
                lnf_ready += lnx_tile_to_lnf(s, g * GT + tq, tq, with_stats=False)
            lnf_all = [(dve, max(lnf_ready))]

            q_evacs = []
            for m in range(8):
                si = m // 4
                spar, sdma = qslot_dma[("Q", s, g, si)]
                b = m % 4
                bi = mmpar[0] % 2
                mmpar[0] += 1
                waits = [sdma] + lnf_all + bank_user.get(bi, [])
                pe_c = None
                for kc in range(FC):
                    pe_c = emit("tensor", waits if kc == 0 else [],
                                lambda e, spar=spar, b=b, kc=kc, bi=bi: e.matmul(
                                    BANKS[bi],
                                    qsl[spar][:, b * 1024 + kc * 128:
                                              b * 1024 + (kc + 1) * 128],
                                    lnf[:, kc * 512:(kc + 1) * 512],
                                    start=(kc == 0), stop=(kc == FC - 1)),
                                inc=(kc == FC - 1))
                a = emit("scalar", [(pe, pe_c)],
                         lambda e, m=m, bi=bi: e.activation(
                             Qg[:, m * 512:(m + 1) * 512], BANKS[bi],
                             AF.Identity, scale=0.125),
                         inc=True)
                if b == 3:
                    emit("scalar", [(pe, pe_c)], lambda e: e.nop(),
                         extra=[qfree])
                bank_user[bi] = [(act, a)]
                q_evacs.append(a)
            proj_all = [(act, max(q_evacs))] + kt_ready

            ser_scope[0] = "att"
            for tq in range(GT):
                qt = g * GT + tq
                ti = s * NT + qt
                L = (qt + 1) * 128
                nk5 = (L + 511) // 512
                emit("sync", [] if ti < 1 else [(dve, xr_free[ti - 1])],
                     lambda e, s=s, qt=qt: e.dma_start(
                         out=xr_t[0],
                         in_=x_d[s * SEQ + qt * 128:s * SEQ + qt * 128 + 128, :]),
                     dma=smdxr)
                xr_dma[ti] = smdxr.n

                norm_last = None
                for h in range(H):
                    hp = (h % 2) * 64
                    hc = h // 2
                    waits = (proj_all + [(act, exp_prev[0]), (dve, prev_row_dve[0])]
                             + bank_user.get(2, []) + bank_user.get(3, []))
                    s_pe = None
                    for c in range(nk5):
                        w = min(512, L - c * 512)
                        s_pe = emit("tensor", waits if c == 0 else [],
                                    lambda e, hp=hp, hc=hc, tq=tq, c=c, w=w: e.matmul(
                                        ar[:, SB + c * 512:SB + c * 512 + w],
                                        Qg[hp:hp + 64, hc * 512 + tq * 128:
                                           hc * 512 + (tq + 1) * 128],
                                        K_fm[hp:hp + 64, hc * SEQ + c * 512:
                                             hc * SEQ + c * 512 + w],
                                        start=True, stop=True),
                                    inc=(c == nk5 - 1))
                    emit("vector", [(pe, s_pe)],
                         lambda e, L=L: e.reduce_max(
                             col(4),
                             ar[:, SB:SB + L].rearrange("p (a b) -> p a b", b=8)[:, :, 0],
                             axis=AX.XY, negate=True),
                         inc=True)
                    d_msk = emit("vector", [],
                                 lambda e, qt=qt: e.tensor_add(
                                     ar[:, SB + qt * 128:SB + (qt + 1) * 128],
                                     ar[:, SB + qt * 128:SB + (qt + 1) * 128],
                                     mask_s),
                                 inc=True)
                    prev_row_dve[0] = d_msk
                    exp_c = [None] * nk5

                    def emit_exp(c):
                        w = min(512, L - c * 512)
                        wp = c % 2
                        waits = [(dve, d_msk)]
                        if wp in win_free:
                            waits.append((pe, win_free[wp]))
                        exp_c[c] = emit("scalar", waits,
                                        lambda e, c=c, w=w, wp=wp: e.activation(
                                            Pw[wp][:, 0:w],
                                            ar[:, SB + c * 512:SB + c * 512 + w],
                                            AF.Exp, bias=col(4),
                                            accum_out=acc[:, c:c + 1]),
                                        inc=True)

                    pt_evac = [None] * nk5
                    pv_last = [None]

                    def emit_pt(c):
                        w = min(512, L - c * 512)
                        ntile = w // 128
                        wp = c % 2
                        waits = [(act, exp_c[c])]
                        if wp in ptb_free:
                            waits.append((dve, ptb_free[wp]))
                        last = None
                        for j in range(ntile):
                            last = emit("tensor", waits if j == 0 else [],
                                        lambda e, wp=wp, j=j: e.transpose(
                                            PTb[wp][:, j * 128:(j + 1) * 128],
                                            Pw[wp][:, j * 128:(j + 1) * 128],
                                            ident_s),
                                        inc=(j == ntile - 1))
                        win_free[wp] = last
                        dwaits = [(pe, last)]
                        if wp in pts_free:
                            dwaits.append((pe, pts_free[wp]))
                        dv = emit("vector", dwaits,
                                  lambda e, wp=wp, w=w: e.tensor_copy(
                                      PTs[wp][:, 0:w], PTb[wp][:, 0:w]),
                                  inc=True)
                        ptb_free[wp] = dv
                        pt_evac[c] = dv

                    def emit_pv(c):
                        w = min(512, L - c * 512)
                        ntile = w // 128
                        wp = c % 2
                        waits = [(dve, pt_evac[c])]
                        if c == 0:
                            waits.append((dve, prev_norm[0]))
                            if prev_epi_act[0]:
                                waits.append((act, prev_epi_act[0]))
                        last = None
                        for j in range(ntile):
                            kt = c * 4 + j
                            st = (c == 0 and j == 0)
                            sp = (c == nk5 - 1 and j == ntile - 1)
                            last = emit("tensor", waits if j == 0 else [],
                                        lambda e, wp=wp, j=j, kt=kt, h=h, st=st,
                                               sp=sp: e.matmul(
                                            O_ps,
                                            PTs[wp][:, j * 128:(j + 1) * 128],
                                            V_tm[:, kt * 1024 + h * 64:
                                                 kt * 1024 + h * 64 + 64],
                                            start=st, stop=sp),
                                        inc=(j == ntile - 1))
                        pts_free[wp] = last
                        pv_last[0] = last

                    emit_exp(0)
                    if nk5 > 1:
                        emit_exp(1)
                    emit_pt(0)
                    if nk5 > 1:
                        if nk5 > 2:
                            emit_exp(2)
                        emit_pt(1)
                        if nk5 > 3:
                            emit_exp(3)
                    emit_pv(0)
                    for c in range(2, nk5):
                        emit_pt(c)
                        emit_pv(c - 1)
                    if nk5 > 1:
                        emit_pv(nk5 - 1)
                    exp_prev[0] = exp_c[-1]

                    if nk5 > 1:
                        dls = emit("vector", [(act, exp_c[-1])],
                                   lambda e, nk5=nk5: e.reduce_sum(
                                       col(5), acc[:, 0:nk5], axis=AX.X),
                                   inc=True)
                        lsrc, lw = col(5), [(dve, dls)]
                    else:
                        lsrc, lw = acc[:, 0:1], [(act, exp_c[-1])]
                    drc = emit("vector", lw,
                               lambda e, lsrc=lsrc: e.reciprocal(col(6), lsrc),
                               inc=True)
                    dnm = emit("vector", [(dve, drc), (pe, pv_last[0])],
                               lambda e, h=h: e.tensor_scalar_mul(
                                   O_bf[:, h * 64:(h + 1) * 64], O_ps, col(6)),
                               inc=True)
                    prev_norm[0] = dnm
                    norm_last = dnm

                ser_scope[0] = "epi"
                a2, d5 = ln_stats(O_bf, [(dve, norm_last)], col(26), col(27))
                aap = emit("scalar", [(act, a2), (dve, d5)],
                           lambda e: e.activation(lntmp, O_bf, AF.Identity,
                                                  bias=col(27), scale=col(26)),
                           inc=True)
                prev_epi_act[0] = aap
                dadd = emit("vector", [(act, aap), (smdxr, xr_dma[ti])],
                            lambda e, tq=tq: e.tensor_add(
                                x2w[:, tq * 1024:(tq + 1) * 1024], lntmp,
                                xr_t[0]),
                            inc=True)
                xr_free[ti] = dadd
                last_dadd[0] = dadd
                x2stat_t[qt] = ln_stats(x2w[:, tq * 1024:(tq + 1) * 1024],
                                        [(dve, dadd)],
                                        r3[:, qt:qt + 1], b3[:, qt:qt + 1])

                if STAGE == "x2":
                    no = nout[0]
                    opar = 0
                    waits = [(dve, dadd)]
                    if no >= 1:
                        waits.append(out_free[no - 1])
                    ac = emit("scalar", waits,
                              lambda e, opar=opar, tq=tq: e.activation(
                                  outt[opar], x2w[:, tq * 1024:(tq + 1) * 1024],
                                  AF.Identity),
                              inc=True)
                    emit("sync", [(act, ac)],
                         lambda e, opar=opar, s=s, qt=qt: e.dma_start(
                             out=out_d[s * SEQ + qt * 128:s * SEQ + qt * 128 + 128, :],
                             in_=outt[opar]),
                         dma=smdout[opar])
                    out_free[no] = (smdout[opar], smdout[opar].n)
                    nout[0] += 1

            bank_user[1] = bank_user.get(1, []) + [(dve, prev_norm[0])]
            bank_user[2] = [(act, exp_prev[0]), (dve, prev_row_dve[0])]
            bank_user[3] = [(act, exp_prev[0]), (dve, prev_row_dve[0])]
            if STAGE == "x2":
                continue

            ser_scope[0] = "bf"
            ln2_ready = []
            for tq in range(GT):
                qt = g * GT + tq
                par = tq % 2
                a2b, d5b = x2stat_t[qt]
                waits = [(act, a2b), (dve, d5b)]
                if par in tmt_free:
                    waits.append(tmt_free[par])
                aap = emit("scalar", waits,
                           lambda e, par=par, tq=tq, qt=qt: e.activation(
                               tmt[par], x2w[:, tq * 1024:(tq + 1) * 1024],
                               AF.Identity, bias=b3[:, qt:qt + 1],
                               scale=r3[:, qt:qt + 1]),
                           inc=True)
                lnfv = lnf.rearrange("p (c t) -> p c t", c=FC)
                outs, lpe = transpose_tile(
                    tmt[par], [(act, aap)],
                    lambda b, tq=tq, lnfv=lnfv: lnfv[:, b * 4:(b + 1) * 4,
                                                     tq * 128:(tq + 1) * 128])
                tmt_free[par] = (pe, lpe)
                ln2_ready += outs
            ln2_all = [(dve, max(ln2_ready))]

            h_in_counts = []
            for mq in range(0, HC, 4):
                p1, c1, i1 = load_wslot([(c_w1, mq + i) for i in range(4)])
                p2, c2, i2 = load_wslot([(c_w2, mq + i) for i in range(4)])
                for b in range(4):
                    m = mq + b
                    biA = 0 if (m % 2 == 0) else 2
                    biB = 1 if (m % 2 == 0) else 3
                    wA = [c1] + ln2_all + bank_user.get(biA, [])
                    wB = [c2] + bank_user.get(biB, [])
                    peA = peB = None
                    for kc in range(FC):
                        peA = emit("tensor", wA if kc == 0 else [],
                                   lambda e, p1=p1, b=b, kc=kc, biA=biA: e.matmul(
                                       BANKS[biA],
                                       wsl[p1][:, b * 1024 + kc * 128:
                                               b * 1024 + (kc + 1) * 128],
                                       lnf[:, kc * 512:(kc + 1) * 512],
                                       start=(kc == 0), stop=(kc == FC - 1)),
                                   inc=(kc == FC - 1))
                    for kc in range(FC):
                        peB = emit("tensor", wB if kc == 0 else [],
                                   lambda e, p2=p2, b=b, kc=kc, biB=biB: e.matmul(
                                       BANKS[biB],
                                       wsl[p2][:, b * 1024 + kc * 128:
                                               b * 1024 + (kc + 1) * 128],
                                       lnf[:, kc * 512:(kc + 1) * 512],
                                       start=(kc == 0), stop=(kc == FC - 1)),
                                   inc=(kc == FC - 1))
                    if b == 3:
                        wslot_free[i1] = peA
                        wslot_free[i2] = peB
                    sp = m % 2
                    aS = emit("scalar", [(pe, peA)],
                              lambda e, m=m, sp=sp, biA=biA: e.activation(
                                  silu_a[sp], BANKS[biA], AF.Silu,
                                  scale=s1p[:, m:m + 1]),
                              inc=True)
                    bank_user[biA] = [(act, aS)]
                    dH = emit("vector", [(act, aS), (pe, peB)],
                              lambda e, m=m, sp=sp, biB=biB: e.scalar_tensor_tensor(
                                  h_in[:, m * 512:(m + 1) * 512], BANKS[biB],
                                  s2p[:, m:m + 1], silu_a[sp],
                                  ALU.mult, ALU.mult),
                              inc=True)
                    bank_user[biB] = [(dve, dH)]
                    h_in_counts.append(dH)
            h_in_all = [(dve, max(h_in_counts))]

            def big_layer(cw, src, dst_fn, scale_p, func, nm, in_ready):
                outs = []
                for m in range(nm):
                    par, cdma, wi = load_wslot([(cw, m)])
                    bi = m % 4
                    w = [cdma] + in_ready + bank_user.get(bi, [])
                    pec = None
                    for kc in range(HC):
                        pec = emit("tensor", w if kc == 0 else [],
                                   lambda e, par=par, kc=kc, bi=bi, src=src: e.matmul(
                                       BANKS[bi],
                                       wsl[par][:, kc * 128:(kc + 1) * 128],
                                       src[:, kc * 512:(kc + 1) * 512],
                                       start=(kc == 0), stop=(kc == HC - 1)),
                                   inc=(kc == HC - 1))
                    wslot_free[wi] = pec
                    a = emit("scalar", [(pe, pec)],
                             lambda e, m=m, bi=bi, sp=scale_p, fn=func,
                                    df=dst_fn: e.activation(
                                 df(m), BANKS[bi], fn, scale=sp[:, m:m + 1]),
                             inc=True)
                    bank_user[bi] = [(act, a)]
                    outs.append(a)
                return outs

            l2 = big_layer(c_h1, h_in, lambda m: h_out[:, m * 512:(m + 1) * 512],
                           sh1p, AF.Gelu, HC, h_in_all)
            l3 = big_layer(c_h2, h_out, lambda m: h_in[:, m * 512:(m + 1) * 512],
                           sh2p, AF.Gelu, HC, [(act, max(l2))])
            l4 = big_layer(c_w3, h_in, lambda m: y_fm[:, m * 512:(m + 1) * 512],
                           s3p, AF.Identity, FC, [(act, max(l3))])
            l4_all = [(act, max(l4))]

            for tq in range(GT):
                qt = g * GT + tq
                par = tq % 2
                outs = []
                waits0 = list(l4_all)
                if par in tmt_free:
                    waits0.append(tmt_free[par])
                for b in range(2):
                    wpp = tp_par[0] % 2
                    tp_par[0] += 1
                    waits = list(waits0)
                    if wpp in ptb_free:
                        waits.append((dve, ptb_free[wpp]))
                    last = None
                    for j in range(4):
                        m = b * 4 + j
                        last = emit("tensor", waits if j == 0 else [],
                                    lambda e, wpp=wpp, j=j, m=m, tq=tq: e.transpose(
                                        PTb[wpp][:, j * 128:(j + 1) * 128],
                                        y_fm[:, m * 512 + tq * 128:
                                             m * 512 + tq * 128 + 128],
                                        ident_s),
                                    inc=(j == 3))
                    dv = emit("vector", [(pe, last)],
                              lambda e, wpp=wpp, par=par, b=b: e.tensor_copy(
                                  tmt[par][:, b * 512:(b + 1) * 512],
                                  PTb[wpp][:, 0:512]),
                              inc=True)
                    ptb_free[wpp] = dv
                    outs.append(dv)
                a2y, d5y = ln_stats(tmt[par], [(dve, max(outs))], col(28), col(29))
                aapy = emit("scalar", [(act, a2y), (dve, d5y)],
                            lambda e, par=par: e.activation(
                                lntmp, tmt[par], AF.Identity,
                                bias=col(29), scale=col(28)),
                            inc=True)
                tmt_free[par] = (act, aapy)
                no = nout[0]
                opar = 0
                waits = [(act, aapy)]
                if no >= 1:
                    waits.append(out_free[no - 1])
                dod = emit("vector", waits,
                           lambda e, opar=opar, tq=tq: e.tensor_add(
                               outt[opar], lntmp,
                               x2w[:, tq * 1024:(tq + 1) * 1024]),
                           inc=True)
                emit("sync", [(dve, dod)],
                     lambda e, opar=opar, s=s, qt=qt: e.dma_start(
                         out=out_d[s * SEQ + qt * 128:s * SEQ + qt * 128 + 128, :],
                         in_=outt[opar]),
                     dma=smdout[opar])
                out_free[no] = (smdout[opar], smdout[opar].n)
                nout[0] += 1

    emit("sync", [(smdout[0], smdout[0].n), (smdout[1], smdout[1].n)],
         lambda e: e.nop())

    with nc.Block() as block:
        for name, attr in (("sync", block.sync), ("gpsimd", block.gpsimd),
                           ("tensor", block.tensor), ("vector", block.vector),
                           ("scalar", block.scalar)):
            def body(e, fns=ops[name]):
                for f in fns:
                    f(e)
            attr(body)

    stack.close()
    return nc


_nc_cache = {}


def _get_nc():
    if STAGE not in _nc_cache:
        _nc_cache[STAGE] = build_nc()
    return _nc_cache[STAGE]


def kernel(L0_attn_x, L0_attn_qkv_weights, L0_ffn_w1_w, L0_ffn_w1_s, L0_ffn_w2_w,
           L0_ffn_w2_s, L0_ffn_h1_w, L0_ffn_h1_s, L0_ffn_h2_w, L0_ffn_h2_s,
           L0_ffn_w3_w, L0_ffn_w3_s):
    x = np.ascontiguousarray(np.asarray(L0_attn_x, dtype=np.float32))
    wqkv = np.ascontiguousarray(np.asarray(L0_attn_qkv_weights, dtype=np.float32))
    w1t = np.ascontiguousarray(np.asarray(L0_ffn_w1_w, dtype=np.float32).T)
    w2t = np.ascontiguousarray(np.asarray(L0_ffn_w2_w, dtype=np.float32).T)
    h1t = np.ascontiguousarray(np.asarray(L0_ffn_h1_w, dtype=np.float32).T)
    h2t = np.ascontiguousarray(np.asarray(L0_ffn_h2_w, dtype=np.float32).T)
    w3t = np.ascontiguousarray(np.asarray(L0_ffn_w3_w, dtype=np.float32).T)

    def pk(sv, nch):
        return np.ascontiguousarray(
            np.asarray(sv, dtype=np.float32).reshape(nch, 128).T)

    maskv = np.triu(np.full((128, 128), -1e9, dtype=np.float32), k=1)
    identv = np.eye(128, dtype=ml_dtypes.bfloat16)

    nc = _get_nc()
    in_maps = []
    for c in range(8):
        in_maps.append(dict(
            x=np.ascontiguousarray(x[2 * c:2 * c + 2].reshape(NSEQ * SEQ, D)),
            wqkv=wqkv, w1t=w1t, w2t=w2t, h1t=h1t, h2t=h2t, w3t=w3t,
            s1p=pk(L0_ffn_w1_s, HC), s2p=pk(L0_ffn_w2_s, HC),
            sh1p=pk(L0_ffn_h1_s, HC), sh2p=pk(L0_ffn_h2_s, HC),
            s3p=pk(L0_ffn_w3_s, FC), mask=maskv, ident=identv,
        ))
    res = run_bass_kernel_spmd(nc, in_maps, core_ids=list(range(8)))
    out = np.concatenate([r["out"].reshape(NSEQ, SEQ, D) for r in res.results],
                         axis=0)
    return out


# version 03
# speedup vs baseline: 1.0606x; 1.0606x over previous
import os
import contextlib
import numpy as np
import ml_dtypes
import concourse.bass as bass
import concourse.mybir as mybir
from concourse.bass_utils import run_bass_kernel_spmd

F32 = mybir.dt.float32
BF16 = mybir.dt.bfloat16
AF = mybir.ActivationFunctionType
AX = mybir.AxisListType
ALU = mybir.AluOpType

SEQ = 2048
D = 1024
H = 16
HID = 4096
NSEQ = 2
NG = 4
GT = 4
NT = NG * GT
FC = D // 128
HC = HID // 128
NGG = NSEQ * NG

STAGE = os.environ.get("MK_STAGE", "full")
SERIAL = os.environ.get("MK_SERIAL", "0")


class Sem:
    def __init__(self, handle):
        self.h = handle
        self.n = 0


def build_nc():
    nc = bass.Bass("TRN2", target_bir_lowering=False)

    x_d = nc.dram_tensor("x", [NSEQ * SEQ, D], F32, kind="ExternalInput").ap()
    wqkv_d = nc.dram_tensor("wqkv", [D, 3 * D], F32, kind="ExternalInput").ap()
    w1t_d = nc.dram_tensor("w1t", [D, HID], F32, kind="ExternalInput").ap()
    w2t_d = nc.dram_tensor("w2t", [D, HID], F32, kind="ExternalInput").ap()
    h1t_d = nc.dram_tensor("h1t", [HID, HID], F32, kind="ExternalInput").ap()
    h2t_d = nc.dram_tensor("h2t", [HID, HID], F32, kind="ExternalInput").ap()
    w3t_d = nc.dram_tensor("w3t", [HID, D], F32, kind="ExternalInput").ap()
    s1p_d = nc.dram_tensor("s1p", [128, HC], F32, kind="ExternalInput").ap()
    s2p_d = nc.dram_tensor("s2p", [128, HC], F32, kind="ExternalInput").ap()
    sh1p_d = nc.dram_tensor("sh1p", [128, HC], F32, kind="ExternalInput").ap()
    sh2p_d = nc.dram_tensor("sh2p", [128, HC], F32, kind="ExternalInput").ap()
    s3p_d = nc.dram_tensor("s3p", [128, FC], F32, kind="ExternalInput").ap()
    mask_d = nc.dram_tensor("mask", [128, 128], F32, kind="ExternalInput").ap()
    ident_d = nc.dram_tensor("ident", [128, 128], BF16, kind="ExternalInput").ap()
    out_d = nc.dram_tensor("out", [NSEQ * SEQ, D], F32, kind="ExternalOutput").ap()
    DUMPKV = os.environ.get("MK_DUMP", "0") == "kv"
    if DUMPKV:
        dumpK = nc.dram_tensor("dumpK", [128, FC * SEQ], BF16, kind="ExternalOutput").ap()
        dumpV = nc.dram_tensor("dumpV", [128, NT * 1024], BF16, kind="ExternalOutput").ap()

    c_w1 = nc.dram_tensor("c_w1", [HC, 128, D], BF16, kind="Internal").ap()
    c_w2 = nc.dram_tensor("c_w2", [HC, 128, D], BF16, kind="Internal").ap()
    c_h1 = nc.dram_tensor("c_h1", [HC, 128, HID], BF16, kind="Internal").ap()
    c_h2 = nc.dram_tensor("c_h2", [HC, 128, HID], BF16, kind="Internal").ap()
    c_w3 = nc.dram_tensor("c_w3", [FC, 128, HID], BF16, kind="Internal").ap()

    A = nc.alloc_sbuf_tensor
    mask_s = A("mask_s", [128, 128], F32)[:]
    ident_s = A("ident_s", [128, 128], BF16)[:]
    s1p = A("s1p_s", [128, HC], F32)[:]
    s2p = A("s2p_s", [128, HC], F32)[:]
    sh1p = A("sh1p_s", [128, HC], F32)[:]
    sh2p = A("sh2p_s", [128, HC], F32)[:]
    s3p = A("s3p_s", [128, FC], F32)[:]
    stats = A("stats", [128, 32], F32)[:]
    r3 = A("rstd3", [128, NT], F32)[:]
    b3 = A("bias3", [128, NT], F32)[:]
    ra1 = A("rstd1", [128, NT], F32)[:]
    rb1 = A("bias1", [128, NT], F32)[:]

    K_fm = A("K_fm", [128, FC * SEQ], BF16)[:]
    V_tm = A("V_tm", [128, NT * 1024], BF16)[:]
    lnf = A("lnf", [128, FC * 512], BF16)[:]
    x2w = A("x2w", [128, GT * 1024], BF16)[:]
    lntmp = A("lntmp", [128, 1024], F32)[:]
    xa_t = [A("xa0", [128, 1024], F32)[:], A("xa1", [128, 1024], F32)[:]]
    xr_t = [A("xr0", [128, 1024], F32)[:]]
    tmt = [A("tmt0", [128, 1024], BF16)[:], A("tmt1", [128, 1024], BF16)[:]]
    outt = [A("outt0", [128, 1024], F32)[:]]
    qsl = [A("qsl0", [128, 4096], BF16)[:], A("qsl1", [128, 4096], BF16)[:]]
    wsl = [A("wsl0", [128, 4096], BF16)[:], A("wsl1", [128, 4096], BF16)[:]]
    ktmp = [A("ktmp0", [128, 1024], BF16)[:], A("ktmp1", [128, 1024], BF16)[:]]
    h_in_t = A("h_in", [128, HC * 512], BF16)
    h_out_t = A("h_out", [128, HC * 512], BF16)

    hi = h_in_t[:]
    Qg = hi[:, 0:4096]
    Pw = [hi[:, 4096:4608], hi[:, 4608:5120]]
    PTsF = hi[:, 7168:9216]
    O_bf = hi[:, 6144:7168]
    ho = h_out_t[:]
    silu_a = [ho[:, 0:1024].bitcast(F32), ho[:, 1024:2048].bitcast(F32)]
    y_fm = ho[:, 2048:6144]
    h_in = hi
    h_out = ho
    junk = lntmp

    assert nc.sbuf_bytes_remaining >= 0, f"SBUF over: {nc.sbuf_bytes_remaining}"

    arena = nc.alloc_psum_tensor("arena", [128, 4096], F32)
    ar = arena[:]
    BANKS = [ar[:, 0:512], ar[:, 512:1024], ar[:, 1024:1536], ar[:, 1536:2048]]
    PTb = [ar[:, 3072:3584].bitcast(BF16), ar[:, 3584:4096].bitcast(BF16)]
    O_ps = ar[:, 512:576]
    SB = 1024

    acc = stats
    col = lambda i: stats[:, i:i + 1]

    ops = {e: [] for e in ("sync", "gpsimd", "tensor", "vector", "scalar")}
    last_wait = {e: {} for e in ops}
    stack = contextlib.ExitStack()
    sm = {}
    for name in ("dx0", "dx1", "dxr", "dout0", "dout1", "dslot0", "dslot1",
                 "dq0", "dq1", "dscr1", "dscr2", "dscrh1", "dscrh2", "dscr3",
                 "dc", "qfree", "pe", "act", "dve"):
        sm[name] = Sem(stack.enter_context(nc.semaphore(name)))
    smdx = [sm["dx0"], sm["dx1"]]
    smdout = [sm["dout0"], sm["dout1"]]
    smdslot = [sm["dslot0"], sm["dslot1"]]
    smdq = [sm["dq0"], sm["dq1"]]
    smdxr, smdc = sm["dxr"], sm["dc"]
    scr_sem = {}
    pe, act, dve = sm["pe"], sm["act"], sm["dve"]
    qfree = sm["qfree"]
    eng_sem = {"tensor": pe, "scalar": act, "vector": dve}
    const_waits = []
    ser_scope = [""]

    def emit(eng, waits, fn, inc=False, dma=None, extra=None):
        lw = last_wait[eng]
        w = []
        allw = list(waits) + const_waits
        ser = (SERIAL == "1"
               or ({"tensor": "pe", "scalar": "act", "vector": "dve"}[eng] in SERIAL
                   if eng in ("tensor", "scalar", "vector") else False)
               or any(t in SERIAL.split(",") for t in ser_scope[0].split()))
        if eng in ("tensor", "scalar", "vector") and ser:
            allw = allw + [(pe, pe.n), (act, act.n), (dve, dve.n)]
        for s, v in allw:
            if v is None or v <= 0:
                continue
            if v > lw.get(id(s), 0):
                w.append((s.h, v))
                lw[id(s)] = v
        tgt = None
        n = None
        if inc:
            s = eng_sem[eng]
            s.n += 1
            n = s.n
            tgt = (s.h, 1)
        elif dma is not None:
            dma.n += 16
            n = dma.n
            tgt = (dma.h, 16)

        xt = []
        if extra:
            for s2 in extra:
                s2.n += 1
                xt.append((s2.h, 1))

        def closure(e, w=w, fn=fn, tgt=tgt, xt=xt):
            for hh, vv in w:
                e.wait_ge(hh, vv)
            inst = fn(e)
            if tgt is not None:
                inst = inst.then_inc(*tgt)
            for hh, vv in xt:
                inst = inst.then_inc(hh, vv)

        ops[eng].append(closure)
        return n

    for dst, src in ((mask_s, mask_d), (ident_s, ident_d), (s1p, s1p_d),
                     (s2p, s2p_d), (sh1p, sh1p_d), (sh2p, sh2p_d), (s3p, s3p_d)):
        emit("sync", [], lambda e, d=dst, s=src: e.dma_start(out=d, in_=s), dma=smdc)
    const_waits.append((smdc, smdc.n))

    def scratch_block(cdst, wsrc, m, nk, sem):
        emit("gpsimd", [],
             lambda e, cd=cdst, ws=wsrc, m=m, nk=nk: e.dma_start(
                 out=cd[m].rearrange("p (c j) -> p c j", c=nk),
                 in_=ws[:, m * 128:(m + 1) * 128].rearrange("(c p) j -> p c j", p=128)),
             dma=sem)

    gsi_ctr = [0]
    qslot_dma = {}

    def _slot_waits():
        gsi = gsi_ctr[0]
        gsi_ctr[0] += 1
        par = gsi % 2
        waits = [(qfree, gsi - 1)] if gsi >= 2 else []
        return gsi, par, waits

    def emit_kv_slot(key, colbase):
        gsi, par, waits = _slot_waits()
        emit("gpsimd", waits,
             lambda e, par=par, colbase=colbase: e.dma_start(
                 out=qsl[par].rearrange("p (c j) -> p c j", c=FC),
                 in_=wqkv_d[:, colbase:colbase + 512].rearrange(
                     "(c p) j -> p c j", p=128)),
             dma=smdq[par])
        qslot_dma[key] = (par, (smdq[par], smdq[par].n))

    def emit_q_slot(key, m0):
        gsi, par, waits = _slot_waits()
        for b in range(4):
            m = m0 + b
            emit("gpsimd", waits if b == 0 else [],
                 lambda e, par=par, b=b, m=m: e.dma_start(
                     out=qsl[par][:, b * 1024:(b + 1) * 1024].rearrange(
                         "p (c j) -> p c j", c=FC),
                     in_=wqkv_d[:, m * 128:(m + 1) * 128].rearrange(
                         "(c p) j -> p c j", p=128)),
                 dma=smdq[par])
        qslot_dma[key] = (par, (smdq[par], smdq[par].n))

    def emit_seq_kv_slots(s):
        for g in range(NG):
            emit_kv_slot(("K", s, g, 0), D + 0)
            emit_kv_slot(("K", s, g, 1), D + 512)
            emit_kv_slot(("V", s, g, 0), 2 * D + 0)
            emit_kv_slot(("V", s, g, 1), 2 * D + 512)

    def emit_seq_q_slots(s):
        for g in range(NG):
            emit_q_slot(("Q", s, g, 0), 0)
            emit_q_slot(("Q", s, g, 1), 4)

    emit_seq_kv_slots(0)
    emit_q_slot(("Q", 0, 0, 0), 0)
    emit_q_slot(("Q", 0, 0, 1), 4)
    for cw, wt, nk, sem in ((c_w1, w1t_d, FC, sm["dscr1"]),
                            (c_w2, w2t_d, FC, sm["dscr2"]),
                            (c_h1, h1t_d, HC, sm["dscrh1"]),
                            (c_h2, h2t_d, HC, sm["dscrh2"]),
                            (c_w3, w3t_d, FC if False else HC, sm["dscr3"])):
        nm = HC if cw is not c_w3 else FC
        for m in range(nm):
            scratch_block(cw, wt, m, nk, sem)
        scr_sem[id(cw)] = (sem, sem.n)
    for g in range(1, NG):
        emit_q_slot(("Q", 0, g, 0), 0)
        emit_q_slot(("Q", 0, g, 1), 4)
    emit_seq_kv_slots(1)
    emit_seq_q_slots(1)

    wslot_i = [0]
    wslot_free = {}

    def load_wslot(blocks):
        par = wslot_i[0] % 2
        waits = []
        if wslot_i[0] >= 2:
            waits.append((pe, wslot_free[wslot_i[0] - 2]))
        waits.append(scr_sem[id(blocks[0][0])])
        if len(blocks) == 1:
            cd, m = blocks[0]
            emit("sync", waits,
                 lambda e, par=par, cd=cd, m=m: e.dma_start(out=wsl[par], in_=cd[m]),
                 dma=smdslot[par])
        else:
            cd, m0 = blocks[0]
            nb = len(blocks)
            emit("sync", waits,
                 lambda e, par=par, cd=cd, m0=m0, nb=nb: e.dma_start(
                     out=wsl[par].rearrange("p (b j) -> p b j", b=nb),
                     in_=cd[m0:m0 + nb].rearrange("b p j -> p b j")),
                 dma=smdslot[par])
        i = wslot_i[0]
        wslot_i[0] += 1
        return par, (smdslot[par], smdslot[par].n), i

    ln_par = [0]

    def ln_stats(src_ap, src_ready, rstd_col, bias_col):
        p8 = 8 + (ln_par[0] % 2) * 8
        ln_par[0] += 1
        ss, nsum, ex2, nm, negvar, varv, lnv = (col(p8 + i) for i in range(7))
        a0 = emit("scalar", src_ready,
                  lambda e, s=src_ap, ss=ss: e.activation(junk, s, AF.Square,
                                                          accum_out=ss),
                  inc=True)
        d0 = emit("vector", src_ready,
                  lambda e, s=src_ap, ns=nsum: e.reduce_sum(ns, s, axis=AX.X,
                                                            negate=True),
                  inc=True)
        emit("vector", [(act, a0)],
             lambda e, ss=ss, ex2=ex2: e.tensor_scalar_mul(ex2, ss, 1.0 / 1024.0),
             inc=True)
        d2 = emit("vector", [(dve, d0)],
                  lambda e, ns=nsum, nm=nm: e.tensor_scalar_mul(nm, ns, 1.0 / 1024.0),
                  inc=True)
        d3 = emit("vector", [(dve, d2)],
                  lambda e, nm=nm, ex2=ex2, nv=negvar: e.scalar_tensor_tensor(
                      nv, nm, nm, ex2, ALU.mult, ALU.subtract),
                  inc=True)
        d4 = emit("vector", [(dve, d3)],
                  lambda e, nv=negvar, vv=varv: e.tensor_scalar(
                      vv, nv, -1.0, 1e-5, ALU.mult, ALU.add),
                  inc=True)
        a1 = emit("scalar", [(dve, d4)],
                  lambda e, vv=varv, lnv=lnv: e.activation(lnv, vv, AF.Ln),
                  inc=True)
        a2 = emit("scalar", [(act, a1)],
                  lambda e, lnv=lnv, r=rstd_col: e.activation(r, lnv, AF.Exp,
                                                              scale=-0.5),
                  inc=True)
        d5 = emit("vector", [(act, a2)],
                  lambda e, nm=nm, r=rstd_col, b=bias_col: e.tensor_mul(b, nm, r),
                  inc=True)
        return a2, d5

    tp_par = [0]
    ptb_free = {}
    tmt_free = {}

    def transpose_tile(src_tile, src_ready, dst_fn):
        outs = []
        last_pe = 0
        for b in range(2):
            par = tp_par[0] % 2
            tp_par[0] += 1
            waits = list(src_ready)
            if par in ptb_free:
                waits.append((dve, ptb_free[par]))
            for j in range(4):
                c = b * 4 + j
                last_pe = emit("tensor", waits if j == 0 else [],
                               lambda e, par=par, j=j, c=c, s=src_tile: e.transpose(
                                   PTb[par][:, j * 128:(j + 1) * 128],
                                   s[:, c * 128:(c + 1) * 128], ident_s),
                               inc=(j == 3))
            dst = dst_fn(b)
            d = emit("vector", [(pe, last_pe)],
                     lambda e, par=par, dst=dst: e.tensor_copy(
                         dst,
                         PTb[par][:, 0:512].rearrange("p (c j) -> p c j", c=4)
                         if len(dst.shape) == 3 else PTb[par][:, 0:512]),
                     inc=True)
            ptb_free[par] = d
            outs.append(d)
        return outs, last_pe

    xa_i = [0]
    xa_free = {}
    last_dadd = [0]

    def load_x_tile(s, qt):
        i = xa_i[0]
        xa_i[0] += 1
        par = i % 2
        waits = [] if i < 2 else [(act, xa_free[i - 2])]
        emit("sync", waits,
             lambda e, par=par, s=s, qt=qt: e.dma_start(
                 out=xa_t[par],
                 in_=x_d[s * SEQ + qt * 128:s * SEQ + qt * 128 + 128, :]),
             dma=smdx[par])
        return i, par, (smdx[par], smdx[par].n)

    def lnx_tile_to_lnf(s, qt, tq, with_stats):
        i, par, dmaw = load_x_tile(s, qt)
        t16 = qt
        if with_stats:
            a2, d5 = ln_stats(xa_t[par], [dmaw, (dve, last_dadd[0])],
                              ra1[:, t16:t16 + 1], rb1[:, t16:t16 + 1])
            waits = [(act, a2), (dve, d5)]
        else:
            waits = [dmaw]
        if par in tmt_free:
            waits.append(tmt_free[par])
        aap = emit("scalar", waits,
                   lambda e, par=par, t16=t16: e.activation(
                       tmt[par], xa_t[par], AF.Identity,
                       bias=rb1[:, t16:t16 + 1], scale=ra1[:, t16:t16 + 1]),
                   inc=True)
        xa_free[i] = aap
        lnfv = lnf.rearrange("p (c t) -> p c t", c=FC)
        outs, lpe = transpose_tile(
            tmt[par], [(act, aap)],
            lambda b, tq=tq, lnfv=lnfv: lnfv[:, b * 4:(b + 1) * 4,
                                             tq * 128:(tq + 1) * 128])
        tmt_free[par] = (pe, lpe)
        return outs

    bank_user = {}
    xr_dma = {}
    xr_free = {}
    out_free = {}
    nout = [0]
    exp_prev = [0]
    prev_row_dve = [0]
    prev_norm = [0]
    prev_epi_act = [0]
    win_free = {}
    last_drc = [0]
    pts_free = {0: 0}
    x2stat_t = {}
    mmpar = [0]
    ktmp_free = {}
    scat_par = [0]

    for s in range(NSEQ):
        ser_scope[0] = "a1 a1ln"
        scat_dve_max = [0]
        scat_act_max = [0]
        v_evac_max = [0]
        for g in range(NG):
            ser_scope[0] = "a1 a1ln"
            lnf_ready = []
            for tq in range(GT):
                lnf_ready += lnx_tile_to_lnf(s, g * GT + tq, tq, with_stats=True)
            lnf_all = [(dve, max(lnf_ready))]

            ser_scope[0] = "a1 a1k"
            for tq in range(GT):
                tt = g * GT + tq
                kpar = tt % 2
                evacs = []
                for ofb in range(2):
                    spar, sdma = qslot_dma[("K", s, g, ofb)]
                    bi = mmpar[0] % 2
                    mmpar[0] += 1
                    waits = [sdma] + lnf_all + bank_user.get(bi, [])
                    if kpar in ktmp_free:
                        waits = waits + [(dve, ktmp_free[kpar][0])]
                    pe_c = None
                    for kc in range(FC):
                        pe_c = emit("tensor", waits if kc == 0 else [],
                                    lambda e, spar=spar, tq=tq, kc=kc, bi=bi: e.matmul(
                                        BANKS[bi],
                                        lnf[:, kc * 512 + tq * 128:
                                            kc * 512 + (tq + 1) * 128],
                                        qsl[spar][:, kc * 512:(kc + 1) * 512],
                                        start=(kc == 0), stop=(kc == FC - 1)),
                                    inc=(kc == FC - 1))
                    aw = [(pe, pe_c)]
                    if kpar in ktmp_free:
                        aw.append((dve, ktmp_free[kpar][0]))
                    a = emit("scalar", aw,
                             lambda e, kpar=kpar, ofb=ofb, bi=bi: e.activation(
                                 ktmp[kpar][:, ofb * 512:(ofb + 1) * 512],
                                 BANKS[bi], AF.Identity),
                             inc=True)
                    if tq == GT - 1:
                        emit("scalar", [(pe, pe_c)], lambda e: e.nop(),
                             extra=[qfree])
                    bank_user[bi] = [(act, a)]
                    evacs.append(a)
                sdve = sact = 0
                ev_w = [(act, max(evacs))]
                for h in range(H):
                    hp = (h % 2) * 64
                    dstv = K_fm[hp:hp + 64,
                                (h // 2) * 2048:(h // 2 + 1) * 2048].rearrange(
                        "p (d r) -> p d r", r=32)
                    for rl in range(2):
                        r = 2 * tt + rl
                        dst = dstv[:, :, r:r + 1]
                        src = ktmp[kpar][rl * 64:rl * 64 + 64,
                                         h * 64:(h + 1) * 64].rearrange(
                            "p (d b) -> p d b", b=1)
                        engn = "vector" if scat_par[0] % 2 == 0 else "scalar"
                        scat_par[0] += 1
                        if engn == "vector":
                            cnt = emit("vector", ev_w,
                                       lambda e, dst=dst, src=src: e.tensor_copy(
                                           dst, src),
                                       inc=True)
                            sdve = cnt
                        else:
                            cnt = emit("scalar", ev_w,
                                       lambda e, dst=dst, src=src: e.copy(dst, src),
                                       inc=True)
                            sact = cnt
                scat_dve_max[0] = max(scat_dve_max[0], sdve)
                scat_act_max[0] = max(scat_act_max[0], sact)
                ktmp_free[kpar] = (sdve, sact)

            ser_scope[0] = "a1 a1v"
            for ofb in range(2):
                spar, sdma = qslot_dma[("V", s, g, ofb)]
                for tq in range(GT):
                    bi = mmpar[0] % 2
                    mmpar[0] += 1
                    waits = [sdma] + lnf_all + bank_user.get(bi, [])
                    pe_c = None
                    for kc in range(FC):
                        pe_c = emit("tensor", waits if kc == 0 else [],
                                    lambda e, spar=spar, tq=tq, kc=kc, bi=bi: e.matmul(
                                        BANKS[bi],
                                        lnf[:, kc * 512 + tq * 128:
                                            kc * 512 + (tq + 1) * 128],
                                        qsl[spar][:, kc * 512:(kc + 1) * 512],
                                        start=(kc == 0), stop=(kc == FC - 1)),
                                    inc=(kc == FC - 1))
                    a = emit("scalar", [(pe, pe_c)],
                             lambda e, g=g, tq=tq, ofb=ofb, bi=bi: e.activation(
                                 V_tm[:, (g * GT + tq) * 1024 + ofb * 512:
                                      (g * GT + tq) * 1024 + (ofb + 1) * 512],
                                 BANKS[bi], AF.Identity),
                             inc=True)
                    if tq == GT - 1:
                        emit("scalar", [(pe, pe_c)], lambda e: e.nop(),
                             extra=[qfree])
                    bank_user[bi] = [(act, a)]
                    v_evac_max[0] = max(v_evac_max[0], a)

        kt_ready = [(dve, scat_dve_max[0]), (act, scat_act_max[0]),
                    (act, v_evac_max[0])]
        if DUMPKV and s == 0:
            emit("sync", kt_ready,
                 lambda e: e.dma_start(out=dumpK, in_=K_fm), dma=sm["dout0"])
            emit("sync", [],
                 lambda e: e.dma_start(out=dumpV, in_=V_tm), dma=sm["dout0"])

        for g in range(NG):
            gg = s * NG + g
            ser_scope[0] = "a2"
            lnf_ready = []
            for tq in range(GT):
                lnf_ready += lnx_tile_to_lnf(s, g * GT + tq, tq, with_stats=False)
            lnf_all = [(dve, max(lnf_ready))]

            q_evacs = []
            for m in range(8):
                si = m // 4
                spar, sdma = qslot_dma[("Q", s, g, si)]
                b = m % 4
                bi = mmpar[0] % 2
                mmpar[0] += 1
                waits = [sdma] + lnf_all + bank_user.get(bi, [])
                pe_c = None
                for kc in range(FC):
                    pe_c = emit("tensor", waits if kc == 0 else [],
                                lambda e, spar=spar, b=b, kc=kc, bi=bi: e.matmul(
                                    BANKS[bi],
                                    qsl[spar][:, b * 1024 + kc * 128:
                                              b * 1024 + (kc + 1) * 128],
                                    lnf[:, kc * 512:(kc + 1) * 512],
                                    start=(kc == 0), stop=(kc == FC - 1)),
                                inc=(kc == FC - 1))
                a = emit("scalar", [(pe, pe_c)],
                         lambda e, m=m, bi=bi: e.activation(
                             Qg[:, m * 512:(m + 1) * 512], BANKS[bi],
                             AF.Identity, scale=0.125),
                         inc=True)
                if b == 3:
                    emit("scalar", [(pe, pe_c)], lambda e: e.nop(),
                         extra=[qfree])
                bank_user[bi] = [(act, a)]
                q_evacs.append(a)
            proj_all = [(act, max(q_evacs))] + kt_ready

            ser_scope[0] = "att"
            for tq in range(GT):
                qt = g * GT + tq
                ti = s * NT + qt
                L = (qt + 1) * 128
                nk5 = (L + 511) // 512
                emit("sync", [] if ti < 1 else [(dve, xr_free[ti - 1])],
                     lambda e, s=s, qt=qt: e.dma_start(
                         out=xr_t[0],
                         in_=x_d[s * SEQ + qt * 128:s * SEQ + qt * 128 + 128, :]),
                     dma=smdxr)
                xr_dma[ti] = smdxr.n

                norm_last = None

                def emit_S(h):
                    hp = (h % 2) * 64
                    hc = h // 2
                    waits = (proj_all + [(act, exp_prev[0]), (dve, prev_row_dve[0])]
                             + bank_user.get(2, []) + bank_user.get(3, []))
                    s_pe = None
                    for c in range(nk5):
                        w = min(512, L - c * 512)
                        s_pe = emit("tensor", waits if c == 0 else [],
                                    lambda e, hp=hp, hc=hc, tq=tq, c=c, w=w: e.matmul(
                                        ar[:, SB + c * 512:SB + c * 512 + w],
                                        Qg[hp:hp + 64, hc * 512 + tq * 128:
                                           hc * 512 + (tq + 1) * 128],
                                        K_fm[hp:hp + 64, hc * SEQ + c * 512:
                                             hc * SEQ + c * 512 + w],
                                        start=True, stop=True),
                                    inc=(c == nk5 - 1))
                    return s_pe

                def emit_softmax_pre(s_pe):
                    emit("vector", [(pe, s_pe)],
                         lambda e, L=L: e.reduce_max(
                             col(4),
                             ar[:, SB:SB + L].rearrange("p (a b) -> p a b", b=8)[:, :, 0],
                             axis=AX.XY, negate=True),
                         inc=True)
                    d_msk = emit("vector", [],
                                 lambda e, qt=qt: e.tensor_add(
                                     ar[:, SB + qt * 128:SB + (qt + 1) * 128],
                                     ar[:, SB + qt * 128:SB + (qt + 1) * 128],
                                     mask_s),
                                 inc=True)
                    prev_row_dve[0] = d_msk
                    return d_msk

                s_pe_next = emit_S(0)
                d_msk_next = emit_softmax_pre(s_pe_next)
                for h in range(H):
                    d_msk = d_msk_next
                    exp_c = [None] * nk5

                    def emit_exp(c):
                        w = min(512, L - c * 512)
                        wp = c % 2
                        waits = [(dve, d_msk), (dve, last_drc[0])]
                        if wp in win_free:
                            waits.append((pe, win_free[wp]))
                        exp_c[c] = emit("scalar", waits,
                                        lambda e, c=c, w=w, wp=wp: e.activation(
                                            Pw[wp][:, 0:w],
                                            ar[:, SB + c * 512:SB + c * 512 + w],
                                            AF.Exp, bias=col(4),
                                            accum_out=acc[:, c:c + 1]),
                                        inc=True)

                    pt_evac = [None] * nk5
                    pv_last = [None]

                    def emit_pt(c):
                        w = min(512, L - c * 512)
                        ntile = w // 128
                        wp = c % 2
                        waits = [(act, exp_c[c])]
                        if wp in ptb_free:
                            waits.append((dve, ptb_free[wp]))
                        last = None
                        for j in range(ntile):
                            last = emit("tensor", waits if j == 0 else [],
                                        lambda e, wp=wp, j=j: e.transpose(
                                            PTb[wp][:, j * 128:(j + 1) * 128],
                                            Pw[wp][:, j * 128:(j + 1) * 128],
                                            ident_s),
                                        inc=(j == ntile - 1))
                        win_free[wp] = last
                        dwaits = [(pe, last), (pe, pts_free[0])]
                        dv = emit("vector", dwaits,
                                  lambda e, wp=wp, w=w, c=c: e.tensor_copy(
                                      PTsF[:, c * 512:c * 512 + w],
                                      PTb[wp][:, 0:w]),
                                  inc=True)
                        ptb_free[wp] = dv
                        pt_evac[c] = dv

                    def emit_pv(c):
                        w = min(512, L - c * 512)
                        ntile = w // 128
                        wp = c % 2
                        waits = [(dve, pt_evac[c])]
                        if c == 0:
                            waits.append((dve, prev_norm[0]))
                            if prev_epi_act[0]:
                                waits.append((act, prev_epi_act[0]))
                        last = None
                        for j in range(ntile):
                            kt = c * 4 + j
                            st = (c == 0 and j == 0)
                            sp = (c == nk5 - 1 and j == ntile - 1)
                            last = emit("tensor", waits if j == 0 else [],
                                        lambda e, c=c, j=j, kt=kt, h=h, st=st,
                                               sp=sp: e.matmul(
                                            O_ps,
                                            PTsF[:, c * 512 + j * 128:
                                                 c * 512 + (j + 1) * 128],
                                            V_tm[:, kt * 1024 + h * 64:
                                                 kt * 1024 + h * 64 + 64],
                                            start=st, stop=sp),
                                        inc=(j == ntile - 1))
                        pv_last[0] = last

                    emit_exp(0)
                    if nk5 > 1:
                        emit_exp(1)
                    emit_pt(0)
                    if nk5 > 1:
                        if nk5 > 2:
                            emit_exp(2)
                        emit_pt(1)
                        if nk5 > 3:
                            emit_exp(3)
                    for c in range(2, nk5):
                        emit_pt(c)
                    exp_prev[0] = exp_c[-1]
                    if h < H - 1:
                        s_pe_next = emit_S(h + 1)
                        d_msk_next = emit_softmax_pre(s_pe_next)
                    for c in range(nk5):
                        emit_pv(c)
                    pts_free[0] = pv_last[0]

                    if nk5 > 1:
                        dls = emit("vector", [(act, exp_c[-1])],
                                   lambda e, nk5=nk5: e.reduce_sum(
                                       col(5), acc[:, 0:nk5], axis=AX.X),
                                   inc=True)
                        lsrc, lw = col(5), [(dve, dls)]
                    else:
                        lsrc, lw = acc[:, 0:1], [(act, exp_c[-1])]
                    drc = emit("vector", lw,
                               lambda e, lsrc=lsrc: e.reciprocal(col(6), lsrc),
                               inc=True)
                    last_drc[0] = drc
                    dnm = emit("vector", [(dve, drc), (pe, pv_last[0])],
                               lambda e, h=h: e.tensor_scalar_mul(
                                   O_bf[:, h * 64:(h + 1) * 64], O_ps, col(6)),
                               inc=True)
                    prev_norm[0] = dnm
                    norm_last = dnm

                ser_scope[0] = "epi"
                a2, d5 = ln_stats(O_bf, [(dve, norm_last)], col(26), col(27))
                aap = emit("scalar", [(act, a2), (dve, d5)],
                           lambda e: e.activation(lntmp, O_bf, AF.Identity,
                                                  bias=col(27), scale=col(26)),
                           inc=True)
                prev_epi_act[0] = aap
                dadd = emit("vector", [(act, aap), (smdxr, xr_dma[ti])],
                            lambda e, tq=tq: e.tensor_add(
                                x2w[:, tq * 1024:(tq + 1) * 1024], lntmp,
                                xr_t[0]),
                            inc=True)
                xr_free[ti] = dadd
                last_dadd[0] = dadd
                x2stat_t[qt] = ln_stats(x2w[:, tq * 1024:(tq + 1) * 1024],
                                        [(dve, dadd)],
                                        r3[:, qt:qt + 1], b3[:, qt:qt + 1])

                if STAGE == "x2":
                    no = nout[0]
                    opar = 0
                    waits = [(dve, dadd)]
                    if no >= 1:
                        waits.append(out_free[no - 1])
                    ac = emit("scalar", waits,
                              lambda e, opar=opar, tq=tq: e.activation(
                                  outt[opar], x2w[:, tq * 1024:(tq + 1) * 1024],
                                  AF.Identity),
                              inc=True)
                    emit("sync", [(act, ac)],
                         lambda e, opar=opar, s=s, qt=qt: e.dma_start(
                             out=out_d[s * SEQ + qt * 128:s * SEQ + qt * 128 + 128, :],
                             in_=outt[opar]),
                         dma=smdout[opar])
                    out_free[no] = (smdout[opar], smdout[opar].n)
                    nout[0] += 1

            bank_user[1] = bank_user.get(1, []) + [(dve, prev_norm[0])]
            bank_user[2] = [(act, exp_prev[0]), (dve, prev_row_dve[0])]
            bank_user[3] = [(act, exp_prev[0]), (dve, prev_row_dve[0])]
            if STAGE == "x2":
                continue

            ser_scope[0] = "bf"
            ln2_ready = []
            for tq in range(GT):
                qt = g * GT + tq
                par = tq % 2
                a2b, d5b = x2stat_t[qt]
                waits = [(act, a2b), (dve, d5b)]
                if par in tmt_free:
                    waits.append(tmt_free[par])
                aap = emit("scalar", waits,
                           lambda e, par=par, tq=tq, qt=qt: e.activation(
                               tmt[par], x2w[:, tq * 1024:(tq + 1) * 1024],
                               AF.Identity, bias=b3[:, qt:qt + 1],
                               scale=r3[:, qt:qt + 1]),
                           inc=True)
                lnfv = lnf.rearrange("p (c t) -> p c t", c=FC)
                outs, lpe = transpose_tile(
                    tmt[par], [(act, aap)],
                    lambda b, tq=tq, lnfv=lnfv: lnfv[:, b * 4:(b + 1) * 4,
                                                     tq * 128:(tq + 1) * 128])
                tmt_free[par] = (pe, lpe)
                ln2_ready += outs
            ln2_all = [(dve, max(ln2_ready))]

            h_in_counts = []
            for mq in range(0, HC, 4):
                p1, c1, i1 = load_wslot([(c_w1, mq + i) for i in range(4)])
                p2, c2, i2 = load_wslot([(c_w2, mq + i) for i in range(4)])
                for b in range(4):
                    m = mq + b
                    biA = 0 if (m % 2 == 0) else 2
                    biB = 1 if (m % 2 == 0) else 3
                    wA = [c1] + ln2_all + bank_user.get(biA, [])
                    wB = [c2] + bank_user.get(biB, [])
                    peA = peB = None
                    for kc in range(FC):
                        peA = emit("tensor", wA if kc == 0 else [],
                                   lambda e, p1=p1, b=b, kc=kc, biA=biA: e.matmul(
                                       BANKS[biA],
                                       wsl[p1][:, b * 1024 + kc * 128:
                                               b * 1024 + (kc + 1) * 128],
                                       lnf[:, kc * 512:(kc + 1) * 512],
                                       start=(kc == 0), stop=(kc == FC - 1)),
                                   inc=(kc == FC - 1))
                    for kc in range(FC):
                        peB = emit("tensor", wB if kc == 0 else [],
                                   lambda e, p2=p2, b=b, kc=kc, biB=biB: e.matmul(
                                       BANKS[biB],
                                       wsl[p2][:, b * 1024 + kc * 128:
                                               b * 1024 + (kc + 1) * 128],
                                       lnf[:, kc * 512:(kc + 1) * 512],
                                       start=(kc == 0), stop=(kc == FC - 1)),
                                   inc=(kc == FC - 1))
                    if b == 3:
                        wslot_free[i1] = peA
                        wslot_free[i2] = peB
                    sp = m % 2
                    aS = emit("scalar", [(pe, peA)],
                              lambda e, m=m, sp=sp, biA=biA: e.activation(
                                  silu_a[sp], BANKS[biA], AF.Silu,
                                  scale=s1p[:, m:m + 1]),
                              inc=True)
                    bank_user[biA] = [(act, aS)]
                    dH = emit("vector", [(act, aS), (pe, peB)],
                              lambda e, m=m, sp=sp, biB=biB: e.scalar_tensor_tensor(
                                  h_in[:, m * 512:(m + 1) * 512], BANKS[biB],
                                  s2p[:, m:m + 1], silu_a[sp],
                                  ALU.mult, ALU.mult),
                              inc=True)
                    bank_user[biB] = [(dve, dH)]
                    h_in_counts.append(dH)
            h_in_all = [(dve, max(h_in_counts))]

            def big_layer(cw, src, dst_fn, scale_p, func, nm, in_ready):
                outs = []
                for m in range(nm):
                    par, cdma, wi = load_wslot([(cw, m)])
                    bi = m % 4
                    w = [cdma] + in_ready + bank_user.get(bi, [])
                    pec = None
                    for kc in range(HC):
                        pec = emit("tensor", w if kc == 0 else [],
                                   lambda e, par=par, kc=kc, bi=bi, src=src: e.matmul(
                                       BANKS[bi],
                                       wsl[par][:, kc * 128:(kc + 1) * 128],
                                       src[:, kc * 512:(kc + 1) * 512],
                                       start=(kc == 0), stop=(kc == HC - 1)),
                                   inc=(kc == HC - 1))
                    wslot_free[wi] = pec
                    a = emit("scalar", [(pe, pec)],
                             lambda e, m=m, bi=bi, sp=scale_p, fn=func,
                                    df=dst_fn: e.activation(
                                 df(m), BANKS[bi], fn, scale=sp[:, m:m + 1]),
                             inc=True)
                    bank_user[bi] = [(act, a)]
                    outs.append(a)
                return outs

            l2 = big_layer(c_h1, h_in, lambda m: h_out[:, m * 512:(m + 1) * 512],
                           sh1p, AF.Gelu, HC, h_in_all)
            l3 = big_layer(c_h2, h_out, lambda m: h_in[:, m * 512:(m + 1) * 512],
                           sh2p, AF.Gelu, HC, [(act, max(l2))])
            l4 = big_layer(c_w3, h_in, lambda m: y_fm[:, m * 512:(m + 1) * 512],
                           s3p, AF.Identity, FC, [(act, max(l3))])
            l4_all = [(act, max(l4))]

            for tq in range(GT):
                qt = g * GT + tq
                par = tq % 2
                outs = []
                waits0 = list(l4_all)
                if par in tmt_free:
                    waits0.append(tmt_free[par])
                for b in range(2):
                    wpp = tp_par[0] % 2
                    tp_par[0] += 1
                    waits = list(waits0)
                    if wpp in ptb_free:
                        waits.append((dve, ptb_free[wpp]))
                    last = None
                    for j in range(4):
                        m = b * 4 + j
                        last = emit("tensor", waits if j == 0 else [],
                                    lambda e, wpp=wpp, j=j, m=m, tq=tq: e.transpose(
                                        PTb[wpp][:, j * 128:(j + 1) * 128],
                                        y_fm[:, m * 512 + tq * 128:
                                             m * 512 + tq * 128 + 128],
                                        ident_s),
                                    inc=(j == 3))
                    dv = emit("vector", [(pe, last)],
                              lambda e, wpp=wpp, par=par, b=b: e.tensor_copy(
                                  tmt[par][:, b * 512:(b + 1) * 512],
                                  PTb[wpp][:, 0:512]),
                              inc=True)
                    ptb_free[wpp] = dv
                    outs.append(dv)
                a2y, d5y = ln_stats(tmt[par], [(dve, max(outs))], col(28), col(29))
                aapy = emit("scalar", [(act, a2y), (dve, d5y)],
                            lambda e, par=par: e.activation(
                                lntmp, tmt[par], AF.Identity,
                                bias=col(29), scale=col(28)),
                            inc=True)
                tmt_free[par] = (act, aapy)
                no = nout[0]
                opar = 0
                waits = [(act, aapy)]
                if no >= 1:
                    waits.append(out_free[no - 1])
                dod = emit("vector", waits,
                           lambda e, opar=opar, tq=tq: e.tensor_add(
                               outt[opar], lntmp,
                               x2w[:, tq * 1024:(tq + 1) * 1024]),
                           inc=True)
                emit("sync", [(dve, dod)],
                     lambda e, opar=opar, s=s, qt=qt: e.dma_start(
                         out=out_d[s * SEQ + qt * 128:s * SEQ + qt * 128 + 128, :],
                         in_=outt[opar]),
                     dma=smdout[opar])
                out_free[no] = (smdout[opar], smdout[opar].n)
                nout[0] += 1

    emit("sync", [(smdout[0], smdout[0].n), (smdout[1], smdout[1].n)],
         lambda e: e.nop())

    with nc.Block() as block:
        for name, attr in (("sync", block.sync), ("gpsimd", block.gpsimd),
                           ("tensor", block.tensor), ("vector", block.vector),
                           ("scalar", block.scalar)):
            def body(e, fns=ops[name]):
                for f in fns:
                    f(e)
            attr(body)

    stack.close()
    return nc


_nc_cache = {}


def _get_nc():
    if STAGE not in _nc_cache:
        _nc_cache[STAGE] = build_nc()
    return _nc_cache[STAGE]


def kernel(L0_attn_x, L0_attn_qkv_weights, L0_ffn_w1_w, L0_ffn_w1_s, L0_ffn_w2_w,
           L0_ffn_w2_s, L0_ffn_h1_w, L0_ffn_h1_s, L0_ffn_h2_w, L0_ffn_h2_s,
           L0_ffn_w3_w, L0_ffn_w3_s):
    x = np.ascontiguousarray(np.asarray(L0_attn_x, dtype=np.float32))
    wqkv = np.ascontiguousarray(np.asarray(L0_attn_qkv_weights, dtype=np.float32))
    w1t = np.ascontiguousarray(np.asarray(L0_ffn_w1_w, dtype=np.float32).T)
    w2t = np.ascontiguousarray(np.asarray(L0_ffn_w2_w, dtype=np.float32).T)
    h1t = np.ascontiguousarray(np.asarray(L0_ffn_h1_w, dtype=np.float32).T)
    h2t = np.ascontiguousarray(np.asarray(L0_ffn_h2_w, dtype=np.float32).T)
    w3t = np.ascontiguousarray(np.asarray(L0_ffn_w3_w, dtype=np.float32).T)

    def pk(sv, nch):
        return np.ascontiguousarray(
            np.asarray(sv, dtype=np.float32).reshape(nch, 128).T)

    maskv = np.triu(np.full((128, 128), -1e9, dtype=np.float32), k=1)
    identv = np.eye(128, dtype=ml_dtypes.bfloat16)

    nc = _get_nc()
    in_maps = []
    for c in range(8):
        in_maps.append(dict(
            x=np.ascontiguousarray(x[2 * c:2 * c + 2].reshape(NSEQ * SEQ, D)),
            wqkv=wqkv, w1t=w1t, w2t=w2t, h1t=h1t, h2t=h2t, w3t=w3t,
            s1p=pk(L0_ffn_w1_s, HC), s2p=pk(L0_ffn_w2_s, HC),
            sh1p=pk(L0_ffn_h1_s, HC), sh2p=pk(L0_ffn_h2_s, HC),
            s3p=pk(L0_ffn_w3_s, FC), mask=maskv, ident=identv,
        ))
    res = run_bass_kernel_spmd(nc, in_maps, core_ids=list(range(8)))
    out = np.concatenate([r["out"].reshape(NSEQ, SEQ, D) for r in res.results],
                         axis=0)
    return out


# version 04
# speedup vs baseline: 1.1220x; 1.0579x over previous
import os
import contextlib
import numpy as np
import ml_dtypes
import concourse.bass as bass
import concourse.mybir as mybir
from concourse.bass_utils import run_bass_kernel_spmd

F32 = mybir.dt.float32
BF16 = mybir.dt.bfloat16
AF = mybir.ActivationFunctionType
AX = mybir.AxisListType
ALU = mybir.AluOpType

SEQ = 2048
D = 1024
H = 16
HID = 4096
NSEQ = 2
NG = 4
GT = 4
NT = NG * GT
FC = D // 128
HC = HID // 128
NGG = NSEQ * NG

STAGE = os.environ.get("MK_STAGE", "full")
SERIAL = os.environ.get("MK_SERIAL", "0")


class Sem:
    def __init__(self, handle):
        self.h = handle
        self.n = 0


def build_nc():
    nc = bass.Bass("TRN2", target_bir_lowering=False)

    x_d = nc.dram_tensor("x", [NSEQ * SEQ, D], F32, kind="ExternalInput").ap()
    wqkv_d = nc.dram_tensor("wqkv", [D, 3 * D], F32, kind="ExternalInput").ap()
    w1t_d = nc.dram_tensor("w1t", [D, HID], F32, kind="ExternalInput").ap()
    w2t_d = nc.dram_tensor("w2t", [D, HID], F32, kind="ExternalInput").ap()
    h1t_d = nc.dram_tensor("h1t", [HID, HID], F32, kind="ExternalInput").ap()
    h2t_d = nc.dram_tensor("h2t", [HID, HID], F32, kind="ExternalInput").ap()
    w3t_d = nc.dram_tensor("w3t", [HID, D], F32, kind="ExternalInput").ap()
    s1p_d = nc.dram_tensor("s1p", [128, HC], F32, kind="ExternalInput").ap()
    s2p_d = nc.dram_tensor("s2p", [128, HC], F32, kind="ExternalInput").ap()
    sh1p_d = nc.dram_tensor("sh1p", [128, HC], F32, kind="ExternalInput").ap()
    sh2p_d = nc.dram_tensor("sh2p", [128, HC], F32, kind="ExternalInput").ap()
    s3p_d = nc.dram_tensor("s3p", [128, FC], F32, kind="ExternalInput").ap()
    mask_d = nc.dram_tensor("mask", [128, 128], F32, kind="ExternalInput").ap()
    ident_d = nc.dram_tensor("ident", [128, 128], BF16, kind="ExternalInput").ap()
    out_d = nc.dram_tensor("out", [NSEQ * SEQ, D], F32, kind="ExternalOutput").ap()
    DUMPKV = os.environ.get("MK_DUMP", "0") == "kv"
    if DUMPKV:
        dumpK = nc.dram_tensor("dumpK", [128, FC * SEQ], BF16, kind="ExternalOutput").ap()
        dumpV = nc.dram_tensor("dumpV", [128, NT * 1024], BF16, kind="ExternalOutput").ap()

    c_w1 = nc.dram_tensor("c_w1", [HC, 128, D], BF16, kind="Internal").ap()
    c_w2 = nc.dram_tensor("c_w2", [HC, 128, D], BF16, kind="Internal").ap()
    c_h1 = nc.dram_tensor("c_h1", [HC, 128, HID], BF16, kind="Internal").ap()
    c_h2 = nc.dram_tensor("c_h2", [HC, 128, HID], BF16, kind="Internal").ap()
    c_w3 = nc.dram_tensor("c_w3", [FC, 128, HID], BF16, kind="Internal").ap()

    A = nc.alloc_sbuf_tensor
    mask_s = A("mask_s", [128, 128], F32)[:]
    ident_s = A("ident_s", [128, 128], BF16)[:]
    s1p = A("s1p_s", [128, HC], F32)[:]
    s2p = A("s2p_s", [128, HC], F32)[:]
    sh1p = A("sh1p_s", [128, HC], F32)[:]
    sh2p = A("sh2p_s", [128, HC], F32)[:]
    s3p = A("s3p_s", [128, FC], F32)[:]
    stats = A("stats", [128, 48], F32)[:]
    r3 = A("rstd3", [128, NT], F32)[:]
    b3 = A("bias3", [128, NT], F32)[:]
    ra1 = A("rstd1", [128, NT], F32)[:]
    rb1 = A("bias1", [128, NT], F32)[:]

    K_fm = A("K_fm", [128, FC * SEQ], BF16)[:]
    V_tm = A("V_tm", [128, NT * 1024], BF16)[:]
    lnf = A("lnf", [128, FC * 512], BF16)[:]
    x2w = A("x2w", [128, GT * 1024], BF16)[:]
    lntmp = A("lntmp", [128, 1024], F32)[:]
    xa_t = [A("xa0", [128, 1024], F32)[:], A("xa1", [128, 1024], F32)[:]]
    xr_t = [A("xr0", [128, 1024], F32)[:]]
    tmt = [A("tmt0", [128, 1024], BF16)[:], A("tmt1", [128, 1024], BF16)[:]]
    outt = [A("outt0", [128, 1024], F32)[:]]
    qsl = [A("qsl0", [128, 4096], BF16)[:], A("qsl1", [128, 4096], BF16)[:]]
    wsl = [A("wsl0", [128, 4096], BF16)[:], A("wsl1", [128, 4096], BF16)[:]]
    ktmp = [A("ktmp0", [128, 1024], BF16)[:], A("ktmp1", [128, 1024], BF16)[:]]
    h_in_t = A("h_in", [128, HC * 512], BF16)
    h_out_t = A("h_out", [128, HC * 512], BF16)

    hi = h_in_t[:]
    Qg = hi[:, 0:4096]
    Pw = [hi[:, 4096:4608], hi[:, 4608:5120]]
    PTsF = hi[:, 7168:9216]
    O_bf = hi[:, 6144:7168]
    ho = h_out_t[:]
    silu_a = [ho[:, 0:1024].bitcast(F32), ho[:, 1024:2048].bitcast(F32)]
    y_fm = ho[:, 2048:6144]
    h_in = hi
    h_out = ho
    junk = lntmp

    assert nc.sbuf_bytes_remaining >= 0, f"SBUF over: {nc.sbuf_bytes_remaining}"

    arena = nc.alloc_psum_tensor("arena", [128, 4096], F32)
    ar = arena[:]
    BANKS = [ar[:, 0:512], ar[:, 512:1024], ar[:, 1024:1536], ar[:, 1536:2048]]
    PTb = [ar[:, 3072:3584].bitcast(BF16), ar[:, 3584:4096].bitcast(BF16)]
    O_ps = ar[:, 512:576]
    SB = 1024

    acc = stats
    col = lambda i: stats[:, i:i + 1]

    ops = {e: [] for e in ("sync", "gpsimd", "tensor", "vector", "scalar")}
    last_wait = {e: {} for e in ops}
    stack = contextlib.ExitStack()
    sm = {}
    for name in ("dx0", "dx1", "dxr", "dout0", "dout1", "dslot0", "dslot1",
                 "dq0", "dq1", "dscr1", "dscr2", "dscrh1", "dscrh2", "dscr3",
                 "dc", "qfree", "pe", "act", "dve"):
        sm[name] = Sem(stack.enter_context(nc.semaphore(name)))
    smdx = [sm["dx0"], sm["dx1"]]
    smdout = [sm["dout0"], sm["dout1"]]
    smdslot = [sm["dslot0"], sm["dslot1"]]
    smdq = [sm["dq0"], sm["dq1"]]
    smdxr, smdc = sm["dxr"], sm["dc"]
    scr_sem = {}
    pe, act, dve = sm["pe"], sm["act"], sm["dve"]
    qfree = sm["qfree"]
    eng_sem = {"tensor": pe, "scalar": act, "vector": dve}
    const_waits = []
    ser_scope = [""]

    def emit(eng, waits, fn, inc=False, dma=None, extra=None):
        lw = last_wait[eng]
        w = []
        allw = list(waits) + const_waits
        ser = (SERIAL == "1"
               or ({"tensor": "pe", "scalar": "act", "vector": "dve"}[eng] in SERIAL
                   if eng in ("tensor", "scalar", "vector") else False)
               or any(t in SERIAL.split(",") for t in ser_scope[0].split()))
        if eng in ("tensor", "scalar", "vector") and ser:
            allw = allw + [(pe, pe.n), (act, act.n), (dve, dve.n)]
        for s, v in allw:
            if v is None or v <= 0:
                continue
            if v > lw.get(id(s), 0):
                w.append((s.h, v))
                lw[id(s)] = v
        tgt = None
        n = None
        if inc:
            s = eng_sem[eng]
            s.n += 1
            n = s.n
            tgt = (s.h, 1)
        elif dma is not None:
            dma.n += 16
            n = dma.n
            tgt = (dma.h, 16)

        xt = []
        if extra:
            for s2 in extra:
                s2.n += 1
                xt.append((s2.h, 1))

        def closure(e, w=w, fn=fn, tgt=tgt, xt=xt):
            for hh, vv in w:
                e.wait_ge(hh, vv)
            inst = fn(e)
            if tgt is not None:
                inst = inst.then_inc(*tgt)
            for hh, vv in xt:
                inst = inst.then_inc(hh, vv)

        ops[eng].append(closure)
        return n

    for dst, src in ((mask_s, mask_d), (ident_s, ident_d), (s1p, s1p_d),
                     (s2p, s2p_d), (sh1p, sh1p_d), (sh2p, sh2p_d), (s3p, s3p_d)):
        emit("sync", [], lambda e, d=dst, s=src: e.dma_start(out=d, in_=s), dma=smdc)
    const_waits.append((smdc, smdc.n))

    def scratch_block(cdst, wsrc, m, nk, sem):
        emit("gpsimd", [],
             lambda e, cd=cdst, ws=wsrc, m=m, nk=nk: e.dma_start(
                 out=cd[m].rearrange("p (c j) -> p c j", c=nk),
                 in_=ws[:, m * 128:(m + 1) * 128].rearrange("(c p) j -> p c j", p=128)),
             dma=sem)

    gsi_ctr = [0]
    qslot_dma = {}

    def _slot_waits():
        gsi = gsi_ctr[0]
        gsi_ctr[0] += 1
        par = gsi % 2
        waits = [(qfree, gsi - 1)] if gsi >= 2 else []
        return gsi, par, waits

    def emit_kv_slot(key, colbase):
        gsi, par, waits = _slot_waits()
        emit("gpsimd", waits,
             lambda e, par=par, colbase=colbase: e.dma_start(
                 out=qsl[par].rearrange("p (c j) -> p c j", c=FC),
                 in_=wqkv_d[:, colbase:colbase + 512].rearrange(
                     "(c p) j -> p c j", p=128)),
             dma=smdq[par])
        qslot_dma[key] = (par, (smdq[par], smdq[par].n))

    def emit_q_slot(key, m0):
        gsi, par, waits = _slot_waits()
        for b in range(4):
            m = m0 + b
            emit("gpsimd", waits if b == 0 else [],
                 lambda e, par=par, b=b, m=m: e.dma_start(
                     out=qsl[par][:, b * 1024:(b + 1) * 1024].rearrange(
                         "p (c j) -> p c j", c=FC),
                     in_=wqkv_d[:, m * 128:(m + 1) * 128].rearrange(
                         "(c p) j -> p c j", p=128)),
                 dma=smdq[par])
        qslot_dma[key] = (par, (smdq[par], smdq[par].n))

    def emit_seq_kv_slots(s):
        for g in range(NG):
            emit_kv_slot(("K", s, g, 0), D + 0)
            emit_kv_slot(("K", s, g, 1), D + 512)
            emit_kv_slot(("V", s, g, 0), 2 * D + 0)
            emit_kv_slot(("V", s, g, 1), 2 * D + 512)

    def emit_seq_q_slots(s):
        for g in range(NG):
            emit_q_slot(("Q", s, g, 0), 0)
            emit_q_slot(("Q", s, g, 1), 4)

    emit_seq_kv_slots(0)
    emit_q_slot(("Q", 0, 0, 0), 0)
    emit_q_slot(("Q", 0, 0, 1), 4)
    for cw, wt, nk, sem in ((c_w1, w1t_d, FC, sm["dscr1"]),
                            (c_w2, w2t_d, FC, sm["dscr2"]),
                            (c_h1, h1t_d, HC, sm["dscrh1"]),
                            (c_h2, h2t_d, HC, sm["dscrh2"]),
                            (c_w3, w3t_d, FC if False else HC, sm["dscr3"])):
        nm = HC if cw is not c_w3 else FC
        for m in range(nm):
            scratch_block(cw, wt, m, nk, sem)
        scr_sem[id(cw)] = (sem, sem.n)
    for g in range(1, NG):
        emit_q_slot(("Q", 0, g, 0), 0)
        emit_q_slot(("Q", 0, g, 1), 4)
    emit_seq_kv_slots(1)
    emit_seq_q_slots(1)

    wslot_i = [0]
    wslot_free = {}

    def load_wslot(blocks):
        par = wslot_i[0] % 2
        waits = []
        if wslot_i[0] >= 2:
            waits.append((pe, wslot_free[wslot_i[0] - 2]))
        waits.append(scr_sem[id(blocks[0][0])])
        if len(blocks) == 1:
            cd, m = blocks[0]
            emit("sync", waits,
                 lambda e, par=par, cd=cd, m=m: e.dma_start(out=wsl[par], in_=cd[m]),
                 dma=smdslot[par])
        else:
            cd, m0 = blocks[0]
            nb = len(blocks)
            emit("sync", waits,
                 lambda e, par=par, cd=cd, m0=m0, nb=nb: e.dma_start(
                     out=wsl[par].rearrange("p (b j) -> p b j", b=nb),
                     in_=cd[m0:m0 + nb].rearrange("b p j -> p b j")),
                 dma=smdslot[par])
        i = wslot_i[0]
        wslot_i[0] += 1
        return par, (smdslot[par], smdslot[par].n), i

    ln_par = [0]

    def ln_stats(src_ap, src_ready, rstd_col, bias_col):
        p8 = 8 + (ln_par[0] % 2) * 8
        ln_par[0] += 1
        ss, nsum, ex2, nm, negvar, varv, lnv = (col(p8 + i) for i in range(7))
        a0 = emit("scalar", src_ready,
                  lambda e, s=src_ap, ss=ss: e.activation(junk, s, AF.Square,
                                                          accum_out=ss),
                  inc=True)
        d0 = emit("vector", src_ready,
                  lambda e, s=src_ap, ns=nsum: e.reduce_sum(ns, s, axis=AX.X,
                                                            negate=True),
                  inc=True)
        emit("vector", [(act, a0)],
             lambda e, ss=ss, ex2=ex2: e.tensor_scalar_mul(ex2, ss, 1.0 / 1024.0),
             inc=True)
        d2 = emit("vector", [(dve, d0)],
                  lambda e, ns=nsum, nm=nm: e.tensor_scalar_mul(nm, ns, 1.0 / 1024.0),
                  inc=True)
        d3 = emit("vector", [(dve, d2)],
                  lambda e, nm=nm, ex2=ex2, nv=negvar: e.scalar_tensor_tensor(
                      nv, nm, nm, ex2, ALU.mult, ALU.subtract),
                  inc=True)
        d4 = emit("vector", [(dve, d3)],
                  lambda e, nv=negvar, vv=varv: e.tensor_scalar(
                      vv, nv, -1.0, 1e-5, ALU.mult, ALU.add),
                  inc=True)
        a1 = emit("scalar", [(dve, d4)],
                  lambda e, vv=varv, lnv=lnv: e.activation(lnv, vv, AF.Ln),
                  inc=True)
        a2 = emit("scalar", [(act, a1)],
                  lambda e, lnv=lnv, r=rstd_col: e.activation(r, lnv, AF.Exp,
                                                              scale=-0.5),
                  inc=True)
        d5 = emit("vector", [(act, a2)],
                  lambda e, nm=nm, r=rstd_col, b=bias_col: e.tensor_mul(b, nm, r),
                  inc=True)
        return a2, d5

    tp_par = [0]
    ptb_free = {}
    tmt_free = {}

    def transpose_tile(src_tile, src_ready, dst_fn):
        outs = []
        last_pe = 0
        for b in range(2):
            par = tp_par[0] % 2
            tp_par[0] += 1
            waits = list(src_ready)
            if par in ptb_free:
                waits.append((dve, ptb_free[par]))
            for j in range(4):
                c = b * 4 + j
                last_pe = emit("tensor", waits if j == 0 else [],
                               lambda e, par=par, j=j, c=c, s=src_tile: e.transpose(
                                   PTb[par][:, j * 128:(j + 1) * 128],
                                   s[:, c * 128:(c + 1) * 128], ident_s),
                               inc=(j == 3))
            dst = dst_fn(b)
            d = emit("vector", [(pe, last_pe)],
                     lambda e, par=par, dst=dst: e.tensor_copy(
                         dst,
                         PTb[par][:, 0:512].rearrange("p (c j) -> p c j", c=4)
                         if len(dst.shape) == 3 else PTb[par][:, 0:512]),
                     inc=True)
            ptb_free[par] = d
            outs.append(d)
        return outs, last_pe

    xa_i = [0]
    xa_free = {}
    last_dadd = [0]

    def load_x_tile(s, qt):
        i = xa_i[0]
        xa_i[0] += 1
        par = i % 2
        waits = [] if i < 2 else [(act, xa_free[i - 2])]
        emit("sync", waits,
             lambda e, par=par, s=s, qt=qt: e.dma_start(
                 out=xa_t[par],
                 in_=x_d[s * SEQ + qt * 128:s * SEQ + qt * 128 + 128, :]),
             dma=smdx[par])
        return i, par, (smdx[par], smdx[par].n)

    def lnx_tile_to_lnf(s, qt, tq, with_stats):
        i, par, dmaw = load_x_tile(s, qt)
        t16 = qt
        if with_stats:
            a2, d5 = ln_stats(xa_t[par], [dmaw, (dve, last_dadd[0])],
                              ra1[:, t16:t16 + 1], rb1[:, t16:t16 + 1])
            waits = [(act, a2), (dve, d5)]
        else:
            waits = [dmaw]
        if par in tmt_free:
            waits.append(tmt_free[par])
        aap = emit("scalar", waits,
                   lambda e, par=par, t16=t16: e.activation(
                       tmt[par], xa_t[par], AF.Identity,
                       bias=rb1[:, t16:t16 + 1], scale=ra1[:, t16:t16 + 1]),
                   inc=True)
        xa_free[i] = aap
        lnfv = lnf.rearrange("p (c t) -> p c t", c=FC)
        outs, lpe = transpose_tile(
            tmt[par], [(act, aap)],
            lambda b, tq=tq, lnfv=lnfv: lnfv[:, b * 4:(b + 1) * 4,
                                             tq * 128:(tq + 1) * 128])
        tmt_free[par] = (pe, lpe)
        return outs

    bank_user = {}
    xr_dma = {}
    xr_free = {}
    out_free = {}
    nout = [0]
    exp_prev = [0]
    prev_row_dve = [0]
    prev_norm = [0]
    prev_epi_act = [0]
    win_free = {}
    last_drc = [0]
    pts_free = {0: 0}
    x2stat_t = {}
    mmpar = [0]
    ktmp_free = {}
    scat_par = [0]

    for s in range(NSEQ):
        ser_scope[0] = "a1 a1ln"
        scat_dve_max = [0]
        scat_act_max = [0]
        v_evac_max = [0]
        for g in range(NG):
            ser_scope[0] = "a1 a1ln"
            lnf_ready = []
            for tq in range(GT):
                lnf_ready += lnx_tile_to_lnf(s, g * GT + tq, tq, with_stats=True)
            lnf_all = [(dve, max(lnf_ready))]

            ser_scope[0] = "a1 a1k"
            for tq in range(GT):
                tt = g * GT + tq
                kpar = tt % 2
                evacs = []
                for ofb in range(2):
                    spar, sdma = qslot_dma[("K", s, g, ofb)]
                    bi = mmpar[0] % 2
                    mmpar[0] += 1
                    waits = [sdma] + lnf_all + bank_user.get(bi, [])
                    if kpar in ktmp_free:
                        waits = waits + [(dve, ktmp_free[kpar][0])]
                    pe_c = None
                    for kc in range(FC):
                        pe_c = emit("tensor", waits if kc == 0 else [],
                                    lambda e, spar=spar, tq=tq, kc=kc, bi=bi: e.matmul(
                                        BANKS[bi],
                                        lnf[:, kc * 512 + tq * 128:
                                            kc * 512 + (tq + 1) * 128],
                                        qsl[spar][:, kc * 512:(kc + 1) * 512],
                                        start=(kc == 0), stop=(kc == FC - 1)),
                                    inc=(kc == FC - 1))
                    aw = [(pe, pe_c)]
                    if kpar in ktmp_free:
                        aw.append((dve, ktmp_free[kpar][0]))
                    a = emit("scalar", aw,
                             lambda e, kpar=kpar, ofb=ofb, bi=bi: e.activation(
                                 ktmp[kpar][:, ofb * 512:(ofb + 1) * 512],
                                 BANKS[bi], AF.Identity),
                             inc=True)
                    if tq == GT - 1:
                        emit("scalar", [(pe, pe_c)], lambda e: e.nop(),
                             extra=[qfree])
                    bank_user[bi] = [(act, a)]
                    evacs.append(a)
                sdve = sact = 0
                ev_w = [(act, max(evacs))]
                kt3 = ktmp[kpar].rearrange("p (hh q d) -> p hh q d", hh=8, q=2)
                for rl in range(2):
                    r = 2 * tt + rl
                    for hpar in range(2):
                        src = kt3[rl * 64:rl * 64 + 64, :, hpar, :]
                        dst = K_fm[hpar * 64:hpar * 64 + 64, :].rearrange(
                            "p (hh d r) -> p hh d r", hh=8, r=32)[:, :, :, r]
                        engn = "vector" if scat_par[0] % 2 == 0 else "scalar"
                        scat_par[0] += 1
                        if engn == "vector":
                            sdve = emit("vector", ev_w,
                                        lambda e, dst=dst, src=src: e.tensor_copy(
                                            dst, src),
                                        inc=True)
                        else:
                            sact = emit("scalar", ev_w,
                                        lambda e, dst=dst, src=src: e.copy(dst, src),
                                        inc=True)
                scat_dve_max[0] = max(scat_dve_max[0], sdve)
                scat_act_max[0] = max(scat_act_max[0], sact)
                ktmp_free[kpar] = (sdve, sact)

            ser_scope[0] = "a1 a1v"
            for ofb in range(2):
                spar, sdma = qslot_dma[("V", s, g, ofb)]
                for tq in range(GT):
                    bi = mmpar[0] % 2
                    mmpar[0] += 1
                    waits = [sdma] + lnf_all + bank_user.get(bi, [])
                    pe_c = None
                    for kc in range(FC):
                        pe_c = emit("tensor", waits if kc == 0 else [],
                                    lambda e, spar=spar, tq=tq, kc=kc, bi=bi: e.matmul(
                                        BANKS[bi],
                                        lnf[:, kc * 512 + tq * 128:
                                            kc * 512 + (tq + 1) * 128],
                                        qsl[spar][:, kc * 512:(kc + 1) * 512],
                                        start=(kc == 0), stop=(kc == FC - 1)),
                                    inc=(kc == FC - 1))
                    a = emit("scalar", [(pe, pe_c)],
                             lambda e, g=g, tq=tq, ofb=ofb, bi=bi: e.activation(
                                 V_tm[:, (g * GT + tq) * 1024 + ofb * 512:
                                      (g * GT + tq) * 1024 + (ofb + 1) * 512],
                                 BANKS[bi], AF.Identity),
                             inc=True)
                    if tq == GT - 1:
                        emit("scalar", [(pe, pe_c)], lambda e: e.nop(),
                             extra=[qfree])
                    bank_user[bi] = [(act, a)]
                    v_evac_max[0] = max(v_evac_max[0], a)

        kt_ready = [(dve, scat_dve_max[0]), (act, scat_act_max[0]),
                    (act, v_evac_max[0])]
        if DUMPKV and s == 0:
            emit("sync", kt_ready,
                 lambda e: e.dma_start(out=dumpK, in_=K_fm), dma=sm["dout0"])
            emit("sync", [],
                 lambda e: e.dma_start(out=dumpV, in_=V_tm), dma=sm["dout0"])

        for g in range(NG):
            gg = s * NG + g
            ser_scope[0] = "a2"
            lnf_ready = []
            for tq in range(GT):
                lnf_ready += lnx_tile_to_lnf(s, g * GT + tq, tq, with_stats=False)
            lnf_all = [(dve, max(lnf_ready))]

            q_evacs = []
            for m in range(8):
                si = m // 4
                spar, sdma = qslot_dma[("Q", s, g, si)]
                b = m % 4
                bi = mmpar[0] % 2
                mmpar[0] += 1
                waits = [sdma] + lnf_all + bank_user.get(bi, [])
                pe_c = None
                for kc in range(FC):
                    pe_c = emit("tensor", waits if kc == 0 else [],
                                lambda e, spar=spar, b=b, kc=kc, bi=bi: e.matmul(
                                    BANKS[bi],
                                    qsl[spar][:, b * 1024 + kc * 128:
                                              b * 1024 + (kc + 1) * 128],
                                    lnf[:, kc * 512:(kc + 1) * 512],
                                    start=(kc == 0), stop=(kc == FC - 1)),
                                inc=(kc == FC - 1))
                a = emit("scalar", [(pe, pe_c)],
                         lambda e, m=m, bi=bi: e.activation(
                             Qg[:, m * 512:(m + 1) * 512], BANKS[bi],
                             AF.Identity, scale=0.125),
                         inc=True)
                if b == 3:
                    emit("scalar", [(pe, pe_c)], lambda e: e.nop(),
                         extra=[qfree])
                bank_user[bi] = [(act, a)]
                q_evacs.append(a)
            proj_all = [(act, max(q_evacs))] + kt_ready

            ser_scope[0] = "att"
            for tq in range(GT):
                qt = g * GT + tq
                ti = s * NT + qt
                L = (qt + 1) * 128
                nk5 = (L + 511) // 512
                emit("sync", [] if ti < 1 else [(dve, xr_free[ti - 1])],
                     lambda e, s=s, qt=qt: e.dma_start(
                         out=xr_t[0],
                         in_=x_d[s * SEQ + qt * 128:s * SEQ + qt * 128 + 128, :]),
                     dma=smdxr)
                xr_dma[ti] = smdxr.n

                norm_last = None

                def emit_S(h):
                    hp = (h % 2) * 64
                    hc = h // 2
                    waits = (proj_all + [(act, exp_prev[0]), (dve, prev_row_dve[0])]
                             + bank_user.get(2, []) + bank_user.get(3, []))
                    s_pe = None
                    for c in range(nk5):
                        w = min(512, L - c * 512)
                        s_pe = emit("tensor", waits if c == 0 else [],
                                    lambda e, hp=hp, hc=hc, tq=tq, c=c, w=w: e.matmul(
                                        ar[:, SB + c * 512:SB + c * 512 + w],
                                        Qg[hp:hp + 64, hc * 512 + tq * 128:
                                           hc * 512 + (tq + 1) * 128],
                                        K_fm[hp:hp + 64, hc * SEQ + c * 512:
                                             hc * SEQ + c * 512 + w],
                                        start=True, stop=True),
                                    inc=(c == nk5 - 1))
                    return s_pe

                def emit_softmax_pre(s_pe):
                    emit("vector", [(pe, s_pe)],
                         lambda e, L=L: e.reduce_max(
                             col(4),
                             ar[:, SB:SB + L].rearrange("p (a b) -> p a b", b=8)[:, :, 0],
                             axis=AX.XY, negate=True),
                         inc=True)
                    d_msk = emit("vector", [],
                                 lambda e, qt=qt: e.tensor_add(
                                     ar[:, SB + qt * 128:SB + (qt + 1) * 128],
                                     ar[:, SB + qt * 128:SB + (qt + 1) * 128],
                                     mask_s),
                                 inc=True)
                    prev_row_dve[0] = d_msk
                    return d_msk

                s_pe_next = emit_S(0)
                d_msk_next = emit_softmax_pre(s_pe_next)
                for h in range(H):
                    d_msk = d_msk_next
                    rp = 0 if h % 2 == 0 else 32
                    exp_c = [None] * nk5

                    def emit_exp(c):
                        w = min(512, L - c * 512)
                        wp = c % 2
                        waits = [(dve, d_msk)]
                        if wp in win_free:
                            waits.append((pe, win_free[wp]))
                        exp_c[c] = emit("scalar", waits,
                                        lambda e, c=c, w=w, wp=wp, rp=rp: e.activation(
                                            Pw[wp][:, 0:w],
                                            ar[:, SB + c * 512:SB + c * 512 + w],
                                            AF.Exp, bias=col(4),
                                            accum_out=stats[:, rp + c:rp + c + 1]),
                                        inc=True)

                    pt_evac = [None] * nk5
                    pv_last = [None]

                    def emit_pt(c):
                        w = min(512, L - c * 512)
                        ntile = w // 128
                        wp = c % 2
                        waits = [(act, exp_c[c])]
                        if wp in ptb_free:
                            waits.append((dve, ptb_free[wp]))
                        last = None
                        for j in range(ntile):
                            last = emit("tensor", waits if j == 0 else [],
                                        lambda e, wp=wp, j=j: e.transpose(
                                            PTb[wp][:, j * 128:(j + 1) * 128],
                                            Pw[wp][:, j * 128:(j + 1) * 128],
                                            ident_s),
                                        inc=(j == ntile - 1))
                        win_free[wp] = last
                        dwaits = [(pe, last), (pe, pts_free[0])]
                        dv = emit("vector", dwaits,
                                  lambda e, wp=wp, w=w, c=c: e.tensor_copy(
                                      PTsF[:, c * 512:c * 512 + w],
                                      PTb[wp][:, 0:w]),
                                  inc=True)
                        ptb_free[wp] = dv
                        pt_evac[c] = dv

                    def emit_pv(c):
                        w = min(512, L - c * 512)
                        ntile = w // 128
                        wp = c % 2
                        waits = [(dve, pt_evac[c])]
                        if c == 0:
                            waits.append((dve, prev_norm[0]))
                            if prev_epi_act[0]:
                                waits.append((act, prev_epi_act[0]))
                        last = None
                        for j in range(ntile):
                            kt = c * 4 + j
                            st = (c == 0 and j == 0)
                            sp = (c == nk5 - 1 and j == ntile - 1)
                            last = emit("tensor", waits if j == 0 else [],
                                        lambda e, c=c, j=j, kt=kt, h=h, st=st,
                                               sp=sp: e.matmul(
                                            O_ps,
                                            PTsF[:, c * 512 + j * 128:
                                                 c * 512 + (j + 1) * 128],
                                            V_tm[:, kt * 1024 + h * 64:
                                                 kt * 1024 + h * 64 + 64],
                                            start=st, stop=sp),
                                        inc=(j == ntile - 1))
                        pv_last[0] = last

                    emit_exp(0)
                    if nk5 > 1:
                        emit_exp(1)
                    emit_pt(0)
                    if nk5 > 1:
                        if nk5 > 2:
                            emit_exp(2)
                        emit_pt(1)
                        if nk5 > 3:
                            emit_exp(3)
                    for c in range(2, nk5):
                        emit_pt(c)
                    exp_prev[0] = exp_c[-1]
                    if h < H - 1:
                        s_pe_next = emit_S(h + 1)
                        d_msk_next = emit_softmax_pre(s_pe_next)
                    for c in range(nk5):
                        emit_pv(c)
                    pts_free[0] = pv_last[0]

                    if nk5 > 1:
                        dls = emit("vector", [(act, exp_c[-1])],
                                   lambda e, nk5=nk5, rp=rp: e.reduce_sum(
                                       stats[:, rp + 5:rp + 6],
                                       stats[:, rp:rp + nk5], axis=AX.X),
                                   inc=True)
                        lsrc, lw = stats[:, rp + 5:rp + 6], [(dve, dls)]
                    else:
                        lsrc, lw = stats[:, rp:rp + 1], [(act, exp_c[-1])]
                    drc = emit("vector", lw,
                               lambda e, lsrc=lsrc, rp=rp: e.reciprocal(
                                   stats[:, rp + 6:rp + 7], lsrc),
                               inc=True)
                    dnm = emit("vector", [(dve, drc), (pe, pv_last[0])],
                               lambda e, h=h, rp=rp: e.tensor_scalar_mul(
                                   O_bf[:, h * 64:(h + 1) * 64], O_ps,
                                   stats[:, rp + 6:rp + 7]),
                               inc=True)
                    prev_norm[0] = dnm
                    norm_last = dnm

                ser_scope[0] = "epi"
                a2, d5 = ln_stats(O_bf, [(dve, norm_last)], col(26), col(27))
                aap = emit("scalar", [(act, a2), (dve, d5)],
                           lambda e: e.activation(lntmp, O_bf, AF.Identity,
                                                  bias=col(27), scale=col(26)),
                           inc=True)
                prev_epi_act[0] = aap
                dadd = emit("vector", [(act, aap), (smdxr, xr_dma[ti])],
                            lambda e, tq=tq: e.tensor_add(
                                x2w[:, tq * 1024:(tq + 1) * 1024], lntmp,
                                xr_t[0]),
                            inc=True)
                xr_free[ti] = dadd
                last_dadd[0] = dadd
                x2stat_t[qt] = ln_stats(x2w[:, tq * 1024:(tq + 1) * 1024],
                                        [(dve, dadd)],
                                        r3[:, qt:qt + 1], b3[:, qt:qt + 1])

                if STAGE == "x2":
                    no = nout[0]
                    opar = 0
                    waits = [(dve, dadd)]
                    if no >= 1:
                        waits.append(out_free[no - 1])
                    ac = emit("scalar", waits,
                              lambda e, opar=opar, tq=tq: e.activation(
                                  outt[opar], x2w[:, tq * 1024:(tq + 1) * 1024],
                                  AF.Identity),
                              inc=True)
                    emit("sync", [(act, ac)],
                         lambda e, opar=opar, s=s, qt=qt: e.dma_start(
                             out=out_d[s * SEQ + qt * 128:s * SEQ + qt * 128 + 128, :],
                             in_=outt[opar]),
                         dma=smdout[opar])
                    out_free[no] = (smdout[opar], smdout[opar].n)
                    nout[0] += 1

            bank_user[1] = bank_user.get(1, []) + [(dve, prev_norm[0])]
            bank_user[2] = [(act, exp_prev[0]), (dve, prev_row_dve[0])]
            bank_user[3] = [(act, exp_prev[0]), (dve, prev_row_dve[0])]
            if STAGE == "x2":
                continue

            ser_scope[0] = "bf"
            ln2_ready = []
            for tq in range(GT):
                qt = g * GT + tq
                par = tq % 2
                a2b, d5b = x2stat_t[qt]
                waits = [(act, a2b), (dve, d5b)]
                if par in tmt_free:
                    waits.append(tmt_free[par])
                aap = emit("scalar", waits,
                           lambda e, par=par, tq=tq, qt=qt: e.activation(
                               tmt[par], x2w[:, tq * 1024:(tq + 1) * 1024],
                               AF.Identity, bias=b3[:, qt:qt + 1],
                               scale=r3[:, qt:qt + 1]),
                           inc=True)
                lnfv = lnf.rearrange("p (c t) -> p c t", c=FC)
                outs, lpe = transpose_tile(
                    tmt[par], [(act, aap)],
                    lambda b, tq=tq, lnfv=lnfv: lnfv[:, b * 4:(b + 1) * 4,
                                                     tq * 128:(tq + 1) * 128])
                tmt_free[par] = (pe, lpe)
                ln2_ready += outs
            ln2_all = [(dve, max(ln2_ready))]

            h_in_counts = []
            for mq in range(0, HC, 4):
                p1, c1, i1 = load_wslot([(c_w1, mq + i) for i in range(4)])
                p2, c2, i2 = load_wslot([(c_w2, mq + i) for i in range(4)])
                for b in range(4):
                    m = mq + b
                    biA = 0 if (m % 2 == 0) else 2
                    biB = 1 if (m % 2 == 0) else 3
                    wA = [c1] + ln2_all + bank_user.get(biA, [])
                    wB = [c2] + bank_user.get(biB, [])
                    peA = peB = None
                    for kc in range(FC):
                        peA = emit("tensor", wA if kc == 0 else [],
                                   lambda e, p1=p1, b=b, kc=kc, biA=biA: e.matmul(
                                       BANKS[biA],
                                       wsl[p1][:, b * 1024 + kc * 128:
                                               b * 1024 + (kc + 1) * 128],
                                       lnf[:, kc * 512:(kc + 1) * 512],
                                       start=(kc == 0), stop=(kc == FC - 1)),
                                   inc=(kc == FC - 1))
                    for kc in range(FC):
                        peB = emit("tensor", wB if kc == 0 else [],
                                   lambda e, p2=p2, b=b, kc=kc, biB=biB: e.matmul(
                                       BANKS[biB],
                                       wsl[p2][:, b * 1024 + kc * 128:
                                               b * 1024 + (kc + 1) * 128],
                                       lnf[:, kc * 512:(kc + 1) * 512],
                                       start=(kc == 0), stop=(kc == FC - 1)),
                                   inc=(kc == FC - 1))
                    if b == 3:
                        wslot_free[i1] = peA
                        wslot_free[i2] = peB
                    sp = m % 2
                    aS = emit("scalar", [(pe, peA)],
                              lambda e, m=m, sp=sp, biA=biA: e.activation(
                                  silu_a[sp], BANKS[biA], AF.Silu,
                                  scale=s1p[:, m:m + 1]),
                              inc=True)
                    bank_user[biA] = [(act, aS)]
                    dH = emit("vector", [(act, aS), (pe, peB)],
                              lambda e, m=m, sp=sp, biB=biB: e.scalar_tensor_tensor(
                                  h_in[:, m * 512:(m + 1) * 512], BANKS[biB],
                                  s2p[:, m:m + 1], silu_a[sp],
                                  ALU.mult, ALU.mult),
                              inc=True)
                    bank_user[biB] = [(dve, dH)]
                    h_in_counts.append(dH)
            h_in_all = [(dve, max(h_in_counts))]

            def big_layer(cw, src, dst_fn, scale_p, func, nm, in_ready):
                outs = []
                for m in range(nm):
                    par, cdma, wi = load_wslot([(cw, m)])
                    bi = m % 4
                    w = [cdma] + in_ready + bank_user.get(bi, [])
                    pec = None
                    for kc in range(HC):
                        pec = emit("tensor", w if kc == 0 else [],
                                   lambda e, par=par, kc=kc, bi=bi, src=src: e.matmul(
                                       BANKS[bi],
                                       wsl[par][:, kc * 128:(kc + 1) * 128],
                                       src[:, kc * 512:(kc + 1) * 512],
                                       start=(kc == 0), stop=(kc == HC - 1)),
                                   inc=(kc == HC - 1))
                    wslot_free[wi] = pec
                    a = emit("scalar", [(pe, pec)],
                             lambda e, m=m, bi=bi, sp=scale_p, fn=func,
                                    df=dst_fn: e.activation(
                                 df(m), BANKS[bi], fn, scale=sp[:, m:m + 1]),
                             inc=True)
                    bank_user[bi] = [(act, a)]
                    outs.append(a)
                return outs

            l2 = big_layer(c_h1, h_in, lambda m: h_out[:, m * 512:(m + 1) * 512],
                           sh1p, AF.Gelu, HC, h_in_all)
            l3 = big_layer(c_h2, h_out, lambda m: h_in[:, m * 512:(m + 1) * 512],
                           sh2p, AF.Gelu, HC, [(act, max(l2))])
            l4 = big_layer(c_w3, h_in, lambda m: y_fm[:, m * 512:(m + 1) * 512],
                           s3p, AF.Identity, FC, [(act, max(l3))])
            l4_all = [(act, max(l4))]

            for tq in range(GT):
                qt = g * GT + tq
                par = tq % 2
                outs = []
                waits0 = list(l4_all)
                if par in tmt_free:
                    waits0.append(tmt_free[par])
                for b in range(2):
                    wpp = tp_par[0] % 2
                    tp_par[0] += 1
                    waits = list(waits0)
                    if wpp in ptb_free:
                        waits.append((dve, ptb_free[wpp]))
                    last = None
                    for j in range(4):
                        m = b * 4 + j
                        last = emit("tensor", waits if j == 0 else [],
                                    lambda e, wpp=wpp, j=j, m=m, tq=tq: e.transpose(
                                        PTb[wpp][:, j * 128:(j + 1) * 128],
                                        y_fm[:, m * 512 + tq * 128:
                                             m * 512 + tq * 128 + 128],
                                        ident_s),
                                    inc=(j == 3))
                    dv = emit("vector", [(pe, last)],
                              lambda e, wpp=wpp, par=par, b=b: e.tensor_copy(
                                  tmt[par][:, b * 512:(b + 1) * 512],
                                  PTb[wpp][:, 0:512]),
                              inc=True)
                    ptb_free[wpp] = dv
                    outs.append(dv)
                a2y, d5y = ln_stats(tmt[par], [(dve, max(outs))], col(28), col(29))
                aapy = emit("scalar", [(act, a2y), (dve, d5y)],
                            lambda e, par=par: e.activation(
                                lntmp, tmt[par], AF.Identity,
                                bias=col(29), scale=col(28)),
                            inc=True)
                tmt_free[par] = (act, aapy)
                no = nout[0]
                opar = 0
                waits = [(act, aapy)]
                if no >= 1:
                    waits.append(out_free[no - 1])
                dod = emit("vector", waits,
                           lambda e, opar=opar, tq=tq: e.tensor_add(
                               outt[opar], lntmp,
                               x2w[:, tq * 1024:(tq + 1) * 1024]),
                           inc=True)
                emit("sync", [(dve, dod)],
                     lambda e, opar=opar, s=s, qt=qt: e.dma_start(
                         out=out_d[s * SEQ + qt * 128:s * SEQ + qt * 128 + 128, :],
                         in_=outt[opar]),
                     dma=smdout[opar])
                out_free[no] = (smdout[opar], smdout[opar].n)
                nout[0] += 1

    emit("sync", [(smdout[0], smdout[0].n), (smdout[1], smdout[1].n)],
         lambda e: e.nop())

    with nc.Block() as block:
        for name, attr in (("sync", block.sync), ("gpsimd", block.gpsimd),
                           ("tensor", block.tensor), ("vector", block.vector),
                           ("scalar", block.scalar)):
            def body(e, fns=ops[name]):
                for f in fns:
                    f(e)
            attr(body)

    stack.close()
    return nc


_nc_cache = {}


def _get_nc():
    if STAGE not in _nc_cache:
        _nc_cache[STAGE] = build_nc()
    return _nc_cache[STAGE]


def kernel(L0_attn_x, L0_attn_qkv_weights, L0_ffn_w1_w, L0_ffn_w1_s, L0_ffn_w2_w,
           L0_ffn_w2_s, L0_ffn_h1_w, L0_ffn_h1_s, L0_ffn_h2_w, L0_ffn_h2_s,
           L0_ffn_w3_w, L0_ffn_w3_s):
    x = np.ascontiguousarray(np.asarray(L0_attn_x, dtype=np.float32))
    wqkv = np.ascontiguousarray(np.asarray(L0_attn_qkv_weights, dtype=np.float32))
    w1t = np.ascontiguousarray(np.asarray(L0_ffn_w1_w, dtype=np.float32).T)
    w2t = np.ascontiguousarray(np.asarray(L0_ffn_w2_w, dtype=np.float32).T)
    h1t = np.ascontiguousarray(np.asarray(L0_ffn_h1_w, dtype=np.float32).T)
    h2t = np.ascontiguousarray(np.asarray(L0_ffn_h2_w, dtype=np.float32).T)
    w3t = np.ascontiguousarray(np.asarray(L0_ffn_w3_w, dtype=np.float32).T)

    def pk(sv, nch):
        return np.ascontiguousarray(
            np.asarray(sv, dtype=np.float32).reshape(nch, 128).T)

    maskv = np.triu(np.full((128, 128), -1e9, dtype=np.float32), k=1)
    identv = np.eye(128, dtype=ml_dtypes.bfloat16)

    nc = _get_nc()
    in_maps = []
    for c in range(8):
        in_maps.append(dict(
            x=np.ascontiguousarray(x[2 * c:2 * c + 2].reshape(NSEQ * SEQ, D)),
            wqkv=wqkv, w1t=w1t, w2t=w2t, h1t=h1t, h2t=h2t, w3t=w3t,
            s1p=pk(L0_ffn_w1_s, HC), s2p=pk(L0_ffn_w2_s, HC),
            sh1p=pk(L0_ffn_h1_s, HC), sh2p=pk(L0_ffn_h2_s, HC),
            s3p=pk(L0_ffn_w3_s, FC), mask=maskv, ident=identv,
        ))
    res = run_bass_kernel_spmd(nc, in_maps, core_ids=list(range(8)))
    out = np.concatenate([r["out"].reshape(NSEQ, SEQ, D) for r in res.results],
                         axis=0)
    return out
